# version 04
# baseline (speedup 1.0000x reference)
import numpy as np
import ml_dtypes
from contextlib import ExitStack

import concourse.bass as bass
import concourse.mybir as mybir
from concourse.bass_utils import run_bass_kernel_spmd

F32 = mybir.dt.float32
AF = mybir.ActivationFunctionType
ALU = mybir.AluOpType
AX = mybir.AxisListType

T = 2048
CD = 2048
NC = 16
H = 16
D = 128
INTER = 8192
NIC = 64
TC = 256
TB = 512
EPS = 1e-5
SCALE = float(1.0 / np.sqrt(np.float64(D)))
NTI = T // TC
NTB = T // TB


def _hadamard(n):
    h = np.array([[1.0]])
    while h.shape[0] < n:
        h = np.block([[h, h], [h, -h]])
    return h.astype(np.float32)


def _host_consts():
    c = {}
    c["h128"] = _hadamard(128)
    c["ih16"] = np.kron(np.eye(8), _hadamard(16)).astype(np.float32)
    c["ih64"] = np.kron(np.eye(2), _hadamard(64)).astype(np.float32)
    c["ident"] = np.eye(128, dtype=np.float32)
    c["h128b"] = c["h128"].astype(np.dtype("bfloat16") if False else np.float32)
    m = np.zeros((128, 128), dtype=np.float32)
    m[np.triu_indices(128, 1)] = -1e30
    c["mask"] = m
    c["onec"] = np.ones((128, 1), dtype=np.float32)
    c["oner"] = np.ones((1, 128), dtype=np.float32)
    inv = 1.0 / 10000.0 ** (np.arange(0, D, 2, dtype=np.float64) / D)
    f = np.arange(T, dtype=np.float64)[:, None] * inv[None, :]
    cosT = np.concatenate([np.cos(f)] * 2, axis=1).T.astype(np.float32)
    sinT = np.concatenate([np.sin(f)] * 2, axis=1).T.astype(np.float32)
    sgn = np.where(np.arange(128) < 64, -1.0, 1.0).astype(np.float32)[:, None]
    c["cosT"] = np.ascontiguousarray(cosT)
    c["sinF"] = np.ascontiguousarray(sinT * sgn)
    return c


def _perm_ug():
    p = np.empty(CD, dtype=np.int64)
    for q in range(16):
        for s in range(8):
            for ch in range(16):
                p[q * 128 + s * 16 + ch] = ch * 128 + 8 * q + s
    return p


def _perm_dn():
    p = np.empty(INTER, dtype=np.int64)
    for q in range(64):
        for s in range(2):
            for ch in range(64):
                p[q * 128 + s * 64 + ch] = ch * 128 + 2 * q + s
    return p


class Sem:
    def __init__(self, h):
        self.h = h
        self.n = 0

    def bump(self, k=1):
        self.n += k
        return (self.h, k)

    def at(self):
        return (self.h, self.n)


class Prog:
    def __init__(self):
        self.ops = []

    def add(self, eng, fn, waits=(), incs=()):
        self.ops.append((eng, fn, tuple(w for w in waits if w is not None), tuple(incs)))

    def replay(self, name, eng):
        for e, fn, waits, incs in self.ops:
            if e != name:
                continue
            for (h, v) in waits:
                eng.wait_ge(h, v)
            ins = fn(eng)
            for (h, k) in incs:
                ins.then_inc(h, k)


def build_program(taps=False):
    nc = bass.Bass("TRN2", target_bir_lowering=False, debug=False)

    dt = lambda n, s, k="ExternalInput": nc.dram_tensor(n, s, F32, kind=k).ap()
    xT = dt("xT", [CD, T])
    wq = dt("wq", [CD, CD])
    wk = dt("wk", [CD, 512])
    wv = dt("wv", [CD, 512])
    wo = dt("wo", [CD, CD])
    wupt = dt("wupt", [CD, INTER])
    wgt = dt("wgt", [CD, INTER])
    wdt = dt("wdt", [INTER, CD])
    su_ug = dt("su_ug", [CD])
    wsug = dt("wsug", [2 * INTER])
    su_dp = dt("su_dp", [INTER])
    wsdp = dt("wsdp", [CD])
    cosT = dt("cosT", [128, T])
    sinF = dt("sinF", [128, T])
    h128_d = dt("h128", [128, 128])
    ih16_d = dt("ih16", [128, 128])
    ih64_d = dt("ih64", [128, 128])
    ident_d = dt("ident", [128, 128])
    mask_d = dt("mask", [128, 128])
    onec_d = dt("onec", [128, 1])
    oner_d = dt("oner", [1, 128])
    y = dt("y", [CD, T], "ExternalOutput")
    ksp = dt("ksp", [4, 128, T], "Internal")
    u1d = dt("u1d", [128, 16 * TC], "Internal")
    u2d = dt("u2d", [128, NIC * TC], "Internal")
    wdb = dt("wdb", [NIC, 128, CD], "Internal")
    h128b = nc.dram_tensor("h128b", [128, 128], mybir.dt.bfloat16, kind="ExternalInput").ap()
    ih64b = nc.dram_tensor("ih64b", [128, 128], mybir.dt.bfloat16, kind="ExternalInput").ap()
    vsp = dt("vsp", [16, 128, 512], "Internal")
    tap = {}
    if taps:
        for nm, sh in [("t_k", [128, 4 * TB]), ("t_v", [128, 4 * TB]),
                       ("t_bz", [128, 16 * TB]), ("t_kraw", [128, 4 * TB]),
                       ("t_qpre", [128, NC * TC]), ("t_rs", [128, 16]),
                       ("t_praw", [128, 256]), ("t_s", [128, 256]),
                       ("t_gb", [128, 512]),
                       ("t_z", [128, NC * TC]), ("t_q", [128, NC * TC]),
                       ("t_p", [128, 256]), ("t_o", [128, NC * TC]),
                       ("t_xm", [128, NC * TC]), ("t_v2", [128, NC * TC]),
                       ("t_g", [128, NIC * TC]), ("t_u2", [128, NIC * TC]),
                       ("t_y2", [128, NIC * TC]), ("t_out", [128, NC * TC]),
                       ("t_stat", [1, 2048])]:
            tap[nm] = dt(nm, sh, "ExternalOutput")

    es = ExitStack()
    sb = lambda n, s: es.enter_context(nc.sbuf_tensor(n, s, F32)).ap()
    arena = sb("arena", [128, 32768])
    xz = sb("xz", [128, NC * TC])
    wbuf = sb("wbuf", [128, 8192])
    suh = sb("suh", [128, NC * 128])
    h128_s = sb("h128s", [128, 128])
    h128b_s = es.enter_context(nc.sbuf_tensor("h128bs", [128, 128], mybir.dt.bfloat16)).ap()
    ih64b_s = es.enter_context(nc.sbuf_tensor("ih64bs", [128, 128], mybir.dt.bfloat16)).ap()
    ih16_s = sb("ih16s", [128, 128])
    ih64_s = sb("ih64s", [128, 128])
    ident_s = sb("idents", [128, 128])
    mask_s = sb("masks", [128, 128])
    onec_s = sb("onecs", [128, 1])
    oner_s = sb("oners", [1, 128])
    suug_s = sb("suugs", [128, 16])
    sudp_s = sb("sudps", [128, 64])
    wsu_s = sb("wsus", [128, 64])
    wsg_s = sb("wsgs", [128, 64])
    wsd_s = sb("wsds", [128, 16])
    rows = sb("rows", [1, 2048])
    mb = sb("mb", [128, 512])
    rb = sb("rb", [128, 512])
    rstat = sb("rstat", [128, 16])
    ptb = sb("ptb", [128, 256])
    gb = sb("gb", [128, 1024])
    stmp = sb("stmp", [128, 256])
    gtmp = sb("gtmp", [128, 256])
    agb = arena[:, 16384:32768].bitcast(mybir.dt.bfloat16)
    aub = arena[:, 0:16384].bitcast(mybir.dt.bfloat16)
    gbb = gb.bitcast(mybir.dt.bfloat16)
    wbufb = wbuf.bitcast(mybir.dt.bfloat16)

    ppA = es.enter_context(nc.psum_tensor("ppA", [128, 512], F32)).ap()
    ppS = es.enter_context(nc.psum_tensor("ppS", [128, 2048], F32)).ap()
    ppT = es.enter_context(nc.psum_tensor("ppT", [128, 512], F32)).ap()
    ppO = es.enter_context(nc.psum_tensor("ppO", [128, 512], F32)).ap()
    ppX = es.enter_context(nc.psum_tensor("ppX", [128, 512], F32)).ap()
    pp = [ppA, ppS[:, 0:512], ppS[:, 512:1024], ppS[:, 1024:1536],
          ppS[:, 1536:2048], ppT, ppO, ppX]

    ld = Sem(es.enter_context(nc.semaphore("ld")))
    gs = [Sem(es.enter_context(nc.semaphore(f"gs{i}"))) for i in range(4)]
    ws = [Sem(es.enter_context(nc.semaphore(f"ws{i}"))) for i in range(2)]
    cl = Sem(es.enter_context(nc.semaphore("cl")))
    kv = Sem(es.enter_context(nc.semaphore("kv")))
    sp = Sem(es.enter_context(nc.semaphore("sp")))
    ud = Sem(es.enter_context(nc.semaphore("ud")))
    wc = Sem(es.enter_context(nc.semaphore("wc")))
    wf = Sem(es.enter_context(nc.semaphore("wf")))
    gq = Sem(es.enter_context(nc.semaphore("gq")))
    st = Sem(es.enter_context(nc.semaphore("st")))
    mm = Sem(es.enter_context(nc.semaphore("mm")))
    ve = Sem(es.enter_context(nc.semaphore("ve")))
    ac = Sem(es.enter_context(nc.semaphore("ac")))

    P = Prog()

    def SY(fn, waits=(), inc=None):
        P.add("sync", fn, waits, (inc.bump(16),) if inc else ())

    def CUSH(eng, waits=()):
        P.add(eng, lambda e: e.nop(cycle_cnt=4000), waits)

    def GD(fn, waits=(), inc=None):
        P.add("gpsimd", fn, waits, (inc.bump(16),) if inc else ())

    def TE(fn, waits=(), inc=None):
        P.add("tensor", fn, waits, (inc.bump(1),) if inc else ())

    def VE(fn, waits=(), inc=None):
        P.add("vector", fn, waits, (inc.bump(1),) if inc else ())

    def AC(fn, waits=(), inc=None):
        P.add("scalar", fn, waits, (inc.bump(1),) if inc else ())

    for dst, src in [(h128_s, h128_d), (ih16_s, ih16_d), (ih64_s, ih64_d),
                     (h128b_s, h128b), (ih64b_s, ih64b),
                     (ident_s, ident_d), (mask_s, mask_d), (onec_s, onec_d),
                     (oner_s, oner_d)]:
        SY(lambda e, d=dst, s=src: e.dma_start(out=d[:], in_=s[:]), inc=ld)
    SY(lambda e: e.dma_start(out=suug_s[:, 0:16],
                             in_=su_ug.rearrange("(c p) -> p c", p=128), allow_slow_non_contiguous=True), inc=ld)
    SY(lambda e: e.dma_start(out=sudp_s[:, 0:64],
                             in_=su_dp.rearrange("(c p) -> p c", p=128), allow_slow_non_contiguous=True), inc=ld)
    SY(lambda e: e.dma_start(out=wsu_s[:, 0:64],
                             in_=wsug[0:INTER].rearrange("(c p) -> p c", p=128), allow_slow_non_contiguous=True), inc=ld)
    SY(lambda e: e.dma_start(out=wsg_s[:, 0:64],
                             in_=wsug[INTER:2 * INTER].rearrange("(c p) -> p c", p=128), allow_slow_non_contiguous=True), inc=ld)
    SY(lambda e: e.dma_start(out=wsd_s[:, 0:16],
                             in_=wsdp.rearrange("(c p) -> p c", p=128), allow_slow_non_contiguous=True), inc=ld)
    setup_ld = ld.at()
    r2048 = float(1.0 / np.sqrt(np.float64(2048.0)))
    r8192 = float(1.0 / np.sqrt(np.float64(8192.0)))
    VE(lambda e: e.tensor_scalar(out=wsu_s[:, 0:64], in0=wsu_s[:, 0:64],
                                 scalar1=r2048, scalar2=None, op0=ALU.mult),
       waits=[setup_ld])
    VE(lambda e: e.tensor_mul(out=wsu_s[:, 0:64], in0=wsu_s[:, 0:64], in1=sudp_s[:, 0:64]))
    VE(lambda e: e.tensor_scalar(out=wsg_s[:, 0:64], in0=wsg_s[:, 0:64],
                                 scalar1=r2048, scalar2=None, op0=ALU.mult))
    VE(lambda e: e.tensor_scalar(out=wsd_s[:, 0:16], in0=wsd_s[:, 0:16],
                                 scalar1=r8192, scalar2=None, op0=ALU.mult))
    for c in range(NC):
        VE(lambda e, c=c: e.tensor_scalar(out=suh[:, c * 128:(c + 1) * 128],
                                          in0=h128_s[:], scalar1=suug_s[:, c:c + 1],
                                          scalar2=None, op0=ALU.mult),
           inc=None)
    VE(lambda e: e.drain(), inc=ve)
    setup_ve = ve.at()

    A_Z = 0
    A_Q = 4096
    A_O = 8192
    A_SC = 12288
    A_KC = 16384
    A_VC = 24576

    def ln_stats(src, width, first_wait, sq0, sq1, sq_wait):
        for c in range(NC):
            TE(lambda e, c=c: e.matmul(pp[0][0:1, 0:width], onec_s[:],
                                       src[:, c * width:(c + 1) * width],
                                       start=(c == 0), stop=(c == NC - 1)),
               waits=first_wait if c == 0 else ())
        sqmm = {}
        for c in range(NC):
            sq = sq0 if c % 2 == 0 else sq1
            w = list(sq_wait) if c < 2 else [sqmm[c - 2]]
            AC(lambda e, sq=sq, c=c: e.activation(out=sq[:, 0:width],
                                                  in_=src[:, c * width:(c + 1) * width],
                                                  func=AF.Square),
               waits=w, inc=ac)
            aq = ac.at()
            TE(lambda e, sq=sq, c=c: e.matmul(pp[1][0:1, 0:width], onec_s[:], sq[:, 0:width],
                                              start=(c == 0), stop=(c == NC - 1)),
               waits=[aq], inc=mm)
            sqmm[c] = mm.at()
        allmm = mm.at()
        VE(lambda e: e.tensor_scalar(out=rows[0:1, 0:width], in0=pp[0][0:1, 0:width],
                                     scalar1=1.0 / CD, scalar2=None, op0=ALU.mult),
           waits=[allmm], inc=ve)
        VE(lambda e: e.tensor_scalar(out=rows[0:1, 512:512 + width], in0=pp[1][0:1, 0:width],
                                     scalar1=1.0 / CD, scalar2=None, op0=ALU.mult))
        VE(lambda e: e.tensor_mul(out=rows[0:1, 1024:1024 + width], in0=rows[0:1, 0:width],
                                  in1=rows[0:1, 0:width]))
        VE(lambda e: e.tensor_sub(out=rows[0:1, 512:512 + width], in0=rows[0:1, 512:512 + width],
                                  in1=rows[0:1, 1024:1024 + width]))
        VE(lambda e: e.tensor_scalar(out=rows[0:1, 512:512 + width], in0=rows[0:1, 512:512 + width],
                                     scalar1=EPS, scalar2=None, op0=ALU.add), inc=ve)
        var_ve = ve.at()
        AC(lambda e: e.activation(out=rows[0:1, 512:512 + width], in_=rows[0:1, 512:512 + width],
                                  func=AF.Sqrt), waits=[var_ve], inc=ac)
        sq_ac = ac.at()
        VE(lambda e: e.reciprocal(out=rows[0:1, 512:512 + width], in_=rows[0:1, 512:512 + width]),
           waits=[sq_ac], inc=ve)
        stats_ve = ve.at()
        TE(lambda e: e.matmul(pp[2][:, 0:width], oner_s[:], rows[0:1, 0:width],
                              start=True, stop=True), waits=[stats_ve], inc=mm)
        m1 = mm.at()
        VE(lambda e: e.tensor_copy(out=mb[:, 0:width], in_=pp[2][:, 0:width]),
           waits=[m1], inc=ve)
        bc1 = ve.at()
        TE(lambda e: e.matmul(pp[2][:, 0:width], oner_s[:], rows[0:1, 512:512 + width],
                              start=True, stop=True), waits=[bc1], inc=mm)
        m2 = mm.at()
        VE(lambda e: e.tensor_copy(out=rb[:, 0:width], in_=pp[2][:, 0:width]),
           waits=[m2], inc=ve)
        return ve.at()

    prev_spill = None
    for tib in range(NTB):
        t0 = tib * TB
        SY(lambda e, t0=t0: e.dma_start(out=arena[:, A_SC + 2048:A_SC + 2560],
                                        in_=cosT[:, t0:t0 + TB]),
           waits=[setup_ld], inc=cl)
        SY(lambda e, t0=t0: e.dma_start(out=arena[:, A_SC + 2560:A_SC + 3072],
                                        in_=sinF[:, t0:t0 + TB]), inc=cl)
        cs_ld = cl.at()
        SY(lambda e, t0=t0: e.dma_start(
            out=arena[:, 0:8192],
            in_=xT.rearrange("(c p) t -> p c t", p=128)[:, :, t0:t0 + TB]),
            waits=[prev_spill], inc=cl)
        x_ld = cl.at()
        bc_ve = ln_stats(arena[:, 0:8192], TB, [x_ld],
                         wbuf[:, 4096:4608], wbuf[:, 4608:5120], [x_ld])
        for c in range(NC):
            VE(lambda e, c=c: e.tensor_sub(out=arena[:, c * TB:(c + 1) * TB],
                                           in0=arena[:, c * TB:(c + 1) * TB], in1=mb[:, 0:TB]),
               waits=[bc_ve] if c == 0 else ())
            VE(lambda e, c=c: e.tensor_mul(out=arena[:, c * TB:(c + 1) * TB],
                                           in0=arena[:, c * TB:(c + 1) * TB], in1=rb[:, 0:TB]),
               inc=ve if c == NC - 1 else None)
        z_ve = ve.at()
        if taps and tib == 0:
            GD(lambda e: e.dma_start(out=tap["t_bz"][:], in_=arena[:, 0:8192]),
               waits=[z_ve], inc=st)
        wkv_done = {}
        for c in range(NC):
            off = 5120 + (c % 2) * 1024
            w = [wkv_done[c - 2]] if c >= 2 else [setup_ld]
            SY(lambda e, c=c, off=off: e.dma_start(out=wbuf[:, off:off + 512],
                                                   in_=wk[c * 128:(c + 1) * 128, :]),
               waits=w, inc=ws[c % 2])
            SY(lambda e, c=c, off=off: e.dma_start(out=wbuf[:, off + 512:off + 1024],
                                                   in_=wv[c * 128:(c + 1) * 128, :]), inc=ws[c % 2])
            wld = ws[c % 2].at()
            for kc in range(4):
                TE(lambda e, c=c, kc=kc, off=off: e.matmul(
                    pp[kc][:, 0:TB], wbuf[:, off + kc * 128:off + (kc + 1) * 128],
                    arena[:, c * TB:(c + 1) * TB],
                    start=(c == 0), stop=(c == NC - 1)),
                    waits=[wld, z_ve] if kc == 0 else ())
            for tb in range(4):
                TE(lambda e, c=c, tb=tb, off=off: e.matmul(
                    pp[4 + tb][:, 0:TB],
                    arena[:, c * TB + tb * 128:c * TB + (tb + 1) * 128],
                    wbuf[:, off + 512:off + 1024],
                    start=(c == 0), stop=(c == NC - 1)),
                    inc=mm if tb == 3 else None)
            wkv_done[c] = mm.at()
        kv_mm = mm.at()
        for kc in range(4):
            VE(lambda e, kc=kc: e.tensor_copy(out=arena[:, 8192 + kc * TB:8192 + (kc + 1) * TB],
                                              in_=pp[kc][:, 0:TB]),
               waits=[kv_mm] if kc == 0 else ())
        for tb in range(4):
            VE(lambda e, tb=tb: e.tensor_copy(out=arena[:, 10240 + tb * TB:10240 + (tb + 1) * TB],
                                              in_=pp[4 + tb][:, 0:TB]),
               inc=ve if tb == 3 else None)
        kv_ve = ve.at()
        if taps and tib == 0:
            GD(lambda e: e.dma_start(out=tap["t_kraw"][:], in_=arena[:, 8192:10240]),
               waits=[kv_ve], inc=st)
        CUSH("gpsimd", [kv_ve])
        GD(lambda e: e.dma_start(out=arena[0:64, A_SC:A_SC + 2048],
                                 in_=arena[64:128, 8192:10240]), inc=gq)
        GD(lambda e: e.dma_start(out=arena[64:128, A_SC:A_SC + 2048],
                                 in_=arena[0:64, 8192:10240]), inc=gq)
        rot_gq = gq.at()
        for kc in range(4):
            VE(lambda e, kc=kc: e.tensor_mul(out=arena[:, 8192 + kc * TB:8192 + (kc + 1) * TB],
                                             in0=arena[:, 8192 + kc * TB:8192 + (kc + 1) * TB],
                                             in1=arena[:, A_SC + 2048:A_SC + 2560]),
               waits=[rot_gq, cs_ld] if kc == 0 else ())
            VE(lambda e, kc=kc: e.tensor_mul(out=arena[:, A_SC + kc * TB:A_SC + (kc + 1) * TB],
                                             in0=arena[:, A_SC + kc * TB:A_SC + (kc + 1) * TB],
                                             in1=arena[:, A_SC + 2560:A_SC + 3072]))
            VE(lambda e, kc=kc: e.tensor_add(out=arena[:, 8192 + kc * TB:8192 + (kc + 1) * TB],
                                             in0=arena[:, 8192 + kc * TB:8192 + (kc + 1) * TB],
                                             in1=arena[:, A_SC + kc * TB:A_SC + (kc + 1) * TB]),
               inc=ve if kc == 3 else None)
        rope_ve = ve.at()
        CUSH("sync", [rope_ve])
        for kc in range(4):
            SY(lambda e, kc=kc, t0=t0: e.dma_start(out=ksp[kc][:, t0:t0 + TB],
                                                   in_=arena[:, 8192 + kc * TB:8192 + (kc + 1) * TB]),
               inc=sp)
        for tb in range(4):
            SY(lambda e, tb=tb, tib=tib: e.dma_start(out=vsp[tib * 4 + tb][:],
                                                     in_=arena[:, 10240 + tb * TB:10240 + (tb + 1) * TB]),
               inc=sp)
        prev_spill = sp.at()
        if taps and tib == 0:
            GD(lambda e: e.dma_start(out=tap["t_k"][:], in_=arena[:, 8192:10240]),
               waits=[rope_ve], inc=st)
            GD(lambda e: e.dma_start(out=tap["t_v"][:], in_=arena[:, 10240:12288]),
               waits=[kv_ve], inc=st)
    spills_done = prev_spill

    prev_store = None
    last_wq_done = None
    wc_total = None
    for ti in range(NTI):
        t0 = ti * TC
        nkk = t0 + TC
        nkb = nkk // 128
        SY(lambda e, t0=t0: e.dma_start(out=gb[:, 0:TC], in_=cosT[:, t0:t0 + TC]),
           waits=[spills_done, prev_store], inc=cl)
        SY(lambda e, t0=t0: e.dma_start(out=gb[:, TC:2 * TC], in_=sinF[:, t0:t0 + TC]),
           inc=cl)
        cs_ld = cl.at()
        SY(lambda e, t0=t0: e.dma_start(
            out=xz[:], in_=xT.rearrange("(c p) t -> p c t", p=128)[:, :, t0:t0 + TC]),
            inc=cl)
        x_ld = cl.at()
        for kc in range(4):
            SY(lambda e, kc=kc, nkk=nkk: e.dma_start(
                out=arena[:, A_KC + kc * 2048:A_KC + kc * 2048 + nkk],
                in_=ksp[kc][:, 0:nkk]), inc=kv)
        for g in range(nkb):
            SY(lambda e, g=g: e.dma_start(
                out=arena[:, A_VC + g * 512:A_VC + (g + 1) * 512], in_=vsp[g][:]), inc=kv)
        kv_ld = kv.at()
        bc_ve = ln_stats(xz, TC, [x_ld], stmp, ptb, [x_ld])
        for c in range(NC):
            VE(lambda e, c=c: e.tensor_sub(out=arena[:, A_Z + c * TC:A_Z + (c + 1) * TC],
                                           in0=xz[:, c * TC:(c + 1) * TC], in1=mb[:, 0:TC]),
               waits=[bc_ve] if c == 0 else ())
            VE(lambda e, c=c: e.tensor_mul(out=arena[:, A_Z + c * TC:A_Z + (c + 1) * TC],
                                           in0=arena[:, A_Z + c * TC:A_Z + (c + 1) * TC],
                                           in1=rb[:, 0:TC]),
               inc=ve if c == NC - 1 else None)
        z_ve = ve.at()
        if taps and ti == 0:
            GD(lambda e: e.dma_start(out=tap["t_z"][:], in_=arena[:, A_Z:A_Z + 4096]),
               waits=[z_ve], inc=st)
            GD(lambda e: e.dma_start(out=tap["t_stat"][0:1, :], in_=rows[0:1, :]),
               waits=[z_ve], inc=st)
        wq_done = {}
        for c in range(NC):
            off = (c % 2) * 2048
            w = [wq_done[c - 2]] if c >= 2 else [last_wq_done]
            SY(lambda e, c=c, off=off: e.dma_start(out=wbuf[:, off:off + 2048],
                                                   in_=wq[c * 128:(c + 1) * 128, :]),
               waits=w, inc=ws[c % 2])
            wld = ws[c % 2].at()
            for qc in range(16):
                TE(lambda e, c=c, qc=qc, off=off: e.matmul(
                    pp[qc // 2][:, (qc % 2) * TC:(qc % 2 + 1) * TC],
                    wbuf[:, off + qc * 128:off + (qc + 1) * 128],
                    arena[:, A_Z + c * TC:A_Z + (c + 1) * TC],
                    start=(c == 0 and qc % 2 == 0), stop=(c == NC - 1),
                    skip_group_check=True),
                    waits=[wld, z_ve] if qc == 0 else (),
                    inc=mm if qc == 15 else None)
            wq_done[c] = mm.at()
        q_mm = mm.at()
        for qc in range(16):
            VE(lambda e, qc=qc: e.tensor_copy(
                out=arena[:, A_Q + qc * TC:A_Q + (qc + 1) * TC],
                in_=pp[qc // 2][:, (qc % 2) * TC:(qc % 2 + 1) * TC]),
                waits=[q_mm] if qc == 0 else (), inc=ve if qc == 15 else None)
        qe_ve = ve.at()
        if taps and ti == 0:
            GD(lambda e: e.dma_start(out=tap["t_qpre"][:], in_=arena[:, A_Q:A_Q + 4096]),
               waits=[qe_ve], inc=st)
        CUSH("gpsimd", [qe_ve, cs_ld])
        GD(lambda e: e.dma_start(out=arena[0:64, A_SC:A_SC + 4096],
                                 in_=arena[64:128, A_Q:A_Q + 4096]), inc=gq)
        GD(lambda e: e.dma_start(out=arena[64:128, A_SC:A_SC + 4096],
                                 in_=arena[0:64, A_Q:A_Q + 4096]), inc=gq)
        rot_gq = gq.at()
        VE(lambda e: e.tensor_scalar(out=gb[:, 0:TC], in0=gb[:, 0:TC],
                                     scalar1=SCALE, scalar2=None, op0=ALU.mult),
           waits=[cs_ld])
        VE(lambda e: e.tensor_scalar(out=gb[:, TC:2 * TC], in0=gb[:, TC:2 * TC],
                                     scalar1=SCALE, scalar2=None, op0=ALU.mult))
        for qc in range(16):
            VE(lambda e, qc=qc: e.tensor_mul(out=arena[:, A_Q + qc * TC:A_Q + (qc + 1) * TC],
                                             in0=arena[:, A_Q + qc * TC:A_Q + (qc + 1) * TC],
                                             in1=gb[:, 0:TC]),
               waits=[rot_gq] if qc == 0 else ())
            VE(lambda e, qc=qc: e.tensor_mul(out=arena[:, A_SC + qc * TC:A_SC + (qc + 1) * TC],
                                             in0=arena[:, A_SC + qc * TC:A_SC + (qc + 1) * TC],
                                             in1=gb[:, TC:2 * TC]))
            VE(lambda e, qc=qc: e.tensor_add(out=arena[:, A_Q + qc * TC:A_Q + (qc + 1) * TC],
                                             in0=arena[:, A_Q + qc * TC:A_Q + (qc + 1) * TC],
                                             in1=arena[:, A_SC + qc * TC:A_SC + (qc + 1) * TC]),
               inc=ve if qc == 15 else None)
        rope_ve = ve.at()
        if taps and ti == 0:
            GD(lambda e: e.dma_start(out=tap["t_q"][:], in_=arena[:, A_Q:A_Q + 4096]),
               waits=[rope_ve], inc=st)
        p_off = [A_SC, A_SC + 2048]
        prev_exp = None
        prev_tp_mm = None
        o_prev = None
        for h in range(H):
            kh = h // 4
            for qb in range(2):
                qi = ti * 2 + qb
                nkq = (qi + 1) * 128
                nj = (nkq + 511) // 512
                for j in range(nj):
                    wd_ = min(512, nkq - j * 512)
                    w = []
                    if h == 0 and qb == 0 and j == 0:
                        w = [rope_ve, kv_ld]
                    if prev_exp is not None and j == 0:
                        w = w + [prev_exp]
                    TE(lambda e, h=h, qb=qb, j=j, wd_=wd_, kh=kh: e.matmul(
                        pp[1 + j][:, 0:wd_],
                        arena[:, A_Q + h * TC + qb * 128:A_Q + h * TC + qb * 128 + 128],
                        arena[:, A_KC + kh * 2048 + j * 512:A_KC + kh * 2048 + j * 512 + wd_],
                        start=True, stop=True),
                        waits=w, inc=mm if j == nj - 1 else None)
                s_mm = mm.at()
                if taps and ti == 0 and h == 0 and qb == 0:
                    VE(lambda e: e.tensor_copy(out=stmp[:, 0:128], in_=ppS[:, 0:128]),
                       waits=[s_mm], inc=ve)
                    GD(lambda e: e.dma_start(out=tap["t_s"][:, 0:128], in_=stmp[:, 0:128]),
                       waits=[ve.at()], inc=st)
                jd = (nkq - 128) // 512
                od = (nkq - 128) % 512
                VE(lambda e, jd=jd, od=od: e.tensor_add(out=pp[1 + jd][:, od:od + 128],
                                                        in0=pp[1 + jd][:, od:od + 128],
                                                        in1=mask_s[:]),
                   waits=[s_mm] + ([prev_exp] if prev_exp else []))
                VE(lambda e, nkq=nkq: e.reduce_max(rstat[:, 0:1], ppS[:, 0:nkq],
                                                   axis=AX.X, negate=True), inc=ve)
                mx_ve = ve.at()
                w = [mx_ve]
                if prev_tp_mm is not None:
                    w = w + [prev_tp_mm]
                AC(lambda e, nkq=nkq, qb=qb: e.activation(
                    out=arena[:, p_off[qb]:p_off[qb] + nkq],
                    in_=ppS[:, 0:nkq], func=AF.Exp,
                    bias=rstat[:, 0:1], scale=1.0,
                    accum_out=rstat[:, 1:2]),
                    waits=w, inc=ac)
                prev_exp = ac.at()
                tap_w = []
                if taps and ti == 0 and h == 0 and qb == 0:
                    GD(lambda e: e.dma_start(out=tap["t_praw"][:, 0:256],
                                             in_=arena[:, A_SC:A_SC + 256]),
                       waits=[prev_exp], inc=st)
                    tap_w = [st.at()]
                VE(lambda e: e.reciprocal(out=rstat[:, 2:3], in_=rstat[:, 1:2]),
                   waits=[prev_exp] + tap_w)
                VE(lambda e: e.drain())
                VE(lambda e, qb=qb, nkq=nkq: e.tensor_scalar(
                    out=arena[:, p_off[qb]:p_off[qb] + nkq],
                    in0=arena[:, p_off[qb]:p_off[qb] + nkq],
                    scalar1=rstat[:, 2:3], scalar2=None, op0=ALU.mult), inc=ve)
                if taps and ti == 0 and h == 0 and qb == 0:
                    GD(lambda e: e.dma_start(out=tap["t_rs"][:], in_=rstat[:]),
                       waits=[ve.at()], inc=st)
            p_ve = ve.at()
            if taps and ti == 0 and h == 0:
                GD(lambda e: e.dma_start(out=tap["t_p"][:, 0:256],
                                         in_=arena[:, A_SC:A_SC + 256]),
                   waits=[p_ve], inc=st)
            nb0 = (ti * 2 + 1)
            pt_ev = None
            for kb in range(nkb):
                both = kb < nb0
                w = [p_ve] if kb == 0 else []
                if pt_ev is not None:
                    w = w + [pt_ev]
                TE(lambda e, kb=kb: e.transpose(
                    pp[5][:, 128:256],
                    arena[:, A_SC + 2048 + kb * 128:A_SC + 2048 + (kb + 1) * 128],
                    ident_s[:]), waits=w, inc=None if both else mm)
                if both:
                    TE(lambda e, kb=kb: e.transpose(
                        pp[5][:, 0:128],
                        arena[:, A_SC + kb * 128:A_SC + (kb + 1) * 128],
                        ident_s[:]), inc=mm)
                t_mm = mm.at()
                VE(lambda e, both=both: e.tensor_copy(
                    out=ptb[:, 0:256] if both else ptb[:, 128:256],
                    in_=pp[5][:, 0:256] if both else pp[5][:, 128:256]),
                    waits=[t_mm], inc=ve)
                pt_ev = ve.at()
                w2 = [pt_ev]
                if kb == 0 and o_prev is not None:
                    w2 = w2 + [o_prev]
                if both:
                    TE(lambda e, kb=kb, kh=kh: e.matmul(
                        pp[6][:, 0:256],
                        arena[:, A_VC + kb * 512 + kh * 128:A_VC + kb * 512 + (kh + 1) * 128],
                        ptb[:, 0:256], start=(kb == 0), stop=(kb == nkb - 1)),
                        waits=w2, inc=mm)
                else:
                    TE(lambda e, kb=kb, kh=kh: e.matmul(
                        pp[6][:, 128:256],
                        arena[:, A_VC + kb * 512 + kh * 128:A_VC + kb * 512 + (kh + 1) * 128],
                        ptb[:, 128:256], start=False, stop=(kb == nkb - 1),
                        skip_group_check=True),
                        waits=w2, inc=mm)
            pv_mm = mm.at()
            prev_tp_mm = pv_mm
            VE(lambda e, h=h: e.tensor_copy(out=arena[:, A_O + h * TC:A_O + (h + 1) * TC],
                                            in_=pp[6][:, 0:256]),
               waits=[pv_mm], inc=ve)
            o_prev = ve.at()
        attn_ve = o_prev
        if taps and ti == 0:
            GD(lambda e: e.dma_start(out=tap["t_o"][:], in_=arena[:, A_O:A_O + 4096]),
               waits=[attn_ve], inc=st)
        wo_done = {}
        for c in range(NC):
            off = (c % 2) * 2048
            w = [wo_done[c - 2]] if c >= 2 else [wq_done[NC - 1]]
            SY(lambda e, c=c, off=off: e.dma_start(out=wbuf[:, off:off + 2048],
                                                   in_=wo[c * 128:(c + 1) * 128, :]),
               waits=w, inc=ws[c % 2])
            wld = ws[c % 2].at()
            for oc in range(16):
                TE(lambda e, c=c, oc=oc, off=off: e.matmul(
                    pp[oc // 2][:, (oc % 2) * TC:(oc % 2 + 1) * TC],
                    wbuf[:, off + oc * 128:off + (oc + 1) * 128],
                    arena[:, A_O + c * TC:A_O + (c + 1) * TC],
                    start=(c == 0 and oc % 2 == 0), stop=(c == NC - 1),
                    skip_group_check=True),
                    waits=[wld, attn_ve] if oc == 0 else (),
                    inc=mm if oc == 15 else None)
            wo_done[c] = mm.at()
        wo_mm = mm.at()
        for oc in range(16):
            VE(lambda e, oc=oc: e.tensor_add(out=xz[:, oc * TC:(oc + 1) * TC],
                                             in0=pp[oc // 2][:, (oc % 2) * TC:(oc % 2 + 1) * TC],
                                             in1=xz[:, oc * TC:(oc + 1) * TC]),
               waits=[wo_mm] if oc == 0 else (), inc=ve if oc == 15 else None)
        xm_ve = ve.at()
        if taps and ti == 0:
            GD(lambda e: e.dma_start(out=tap["t_xm"][:], in_=xz[:]),
               waits=[xm_ve], inc=st)

        bc2_ve = ln_stats(xz, TC, [xm_ve], stmp, ptb, [xm_ve])
        for c in range(NC):
            VE(lambda e, c=c: e.tensor_sub(out=arena[:, A_Z + c * TC:A_Z + (c + 1) * TC],
                                           in0=xz[:, c * TC:(c + 1) * TC], in1=mb[:, 0:TC]),
               waits=[bc2_ve] if c == 0 else ())
            VE(lambda e, c=c: e.tensor_mul(out=arena[:, A_Z + c * TC:A_Z + (c + 1) * TC],
                                           in0=arena[:, A_Z + c * TC:A_Z + (c + 1) * TC],
                                           in1=rb[:, 0:TC]),
               inc=ve if c == NC - 1 else None)
        z2_ve = ve.at()
        sa_ev = {}
        for c in range(NC):
            bank = c % 2
            w = [z2_ve] if c == 0 else []
            if c >= 2:
                w = w + [sa_ev[c - 2]]
            TE(lambda e, c=c, bank=bank: e.matmul(pp[bank][:, 0:TC],
                                                  suh[:, c * 128:(c + 1) * 128],
                                                  arena[:, A_Z + c * TC:A_Z + (c + 1) * TC],
                                                  start=True, stop=True),
               waits=w, inc=mm)
            cm = mm.at()
            VE(lambda e, c=c, bank=bank: e.tensor_copy(
                out=arena[:, A_Q + c * TC:A_Q + (c + 1) * TC], in_=pp[bank][:, 0:TC]),
                waits=[cm], inc=ve)
            sa_ev[c] = ve.at()
        VE(lambda e: e.drain(), inc=ve)
        u1_ve = ve.at()
        sb_ev = {}
        CUSH("sync", [u1_ve])
        SY(lambda e: e.dma_start(out=u1d[:], in_=arena[:, A_Q:A_Q + 4096]), inc=ud)
        u1_ld = ud.at()
        g_gq16 = {}
        def em_gather16(q):
            slot = (q % 4) * 256
            w = [u1_ld] if q == 0 else []
            if q >= 4:
                w = w + [sb_ev[q - 4]]
            SY(lambda e, q=q, slot=slot: e.dma_start(
                out=gb[:, slot:slot + 256],
                in_=u1d[8 * q:8 * q + 8, :].rearrange("s (c t) -> s c t", t=TC)),
                waits=w, inc=gs[q % 4])
            g_gq16[q] = gs[q % 4].at()
        for q in range(3):
            em_gather16(q)
        for q in range(16):
            slot = (q % 4) * 256
            if q + 3 < 16:
                em_gather16(q + 3)
            wmm = [g_gq16[q]]
            if q >= 2:
                wmm = wmm + [sb_ev[q - 2]]
            if q < 4:
                CUSH("tensor", [g_gq16[q]])
            TE(lambda e, q=q, slot=slot: e.matmul(pp[2 + q % 2][:, 0:TC], ih16_s[:],
                                                  gb[:, slot:slot + 256],
                                                  start=True, stop=True),
               waits=wmm, inc=mm)
            qm = mm.at()
            VE(lambda e, q=q: e.tensor_copy(out=arena[:, A_O + q * TC:A_O + (q + 1) * TC],
                                            in_=pp[2 + q % 2][:, 0:TC]),
               waits=[qm], inc=ve)
            sb_ev[q] = ve.at()
        v2_ve = ve.at()
        if taps and ti == 0:
            GD(lambda e: e.dma_start(out=tap["t_v2"][:], in_=arena[:, A_O:A_O + 4096]),
               waits=[v2_ve], inc=st)
        ug_ev = {}
        for ic in range(NIC):
            off = (ic % 2) * 4096
            w = [wo_done[NC - 1]] if ic < 2 else [ug_ev[ic - 2]]
            SY(lambda e, ic=ic, off=off: e.dma_start(
                out=wbuf[:, off:off + 2048],
                in_=wupt.rearrange("(c p) m -> p c m", p=128)[:, :, ic * 128:(ic + 1) * 128]),
                waits=w, inc=ws[ic % 2])
            SY(lambda e, ic=ic, off=off: e.dma_start(
                out=wbuf[:, off + 2048:off + 4096],
                in_=wgt.rearrange("(c p) m -> p c m", p=128)[:, :, ic * 128:(ic + 1) * 128]),
                inc=ws[ic % 2])
            wld = ws[ic % 2].at()
            bu, bg = (4, 5) if ic % 2 == 0 else (6, 7)
            wmm = [v2_ve, wld]
            if ic >= 2:
                wmm = wmm + [ug_ev[ic - 2]]
            for c in range(NC):
                TE(lambda e, c=c, off=off, bu=bu: e.matmul(
                    pp[bu][:, 0:TC], wbuf[:, off + c * 128:off + (c + 1) * 128],
                    arena[:, A_O + c * TC:A_O + (c + 1) * TC],
                    start=(c == 0), stop=(c == NC - 1)),
                    waits=wmm if c == 0 else ())
                TE(lambda e, c=c, off=off, bg=bg: e.matmul(
                    pp[bg][:, 0:TC], wbuf[:, off + 2048 + c * 128:off + 2048 + (c + 1) * 128],
                    arena[:, A_O + c * TC:A_O + (c + 1) * TC],
                    start=(c == 0), stop=(c == NC - 1)),
                    inc=mm if c == NC - 1 else None)
            ug_mm = mm.at()
            AC(lambda e, ic=ic, bg=bg: e.activation(out=stmp[:], in_=pp[bg][:, 0:TC],
                                                    func=AF.Silu, scale=wsg_s[:, ic:ic + 1]),
               waits=[ug_mm], inc=ac)
            sl_ac = ac.at()
            VE(lambda e, ic=ic, bu=bu: e.scalar_tensor_tensor(
                out=gtmp[:], in0=pp[bu][:, 0:TC], scalar=wsu_s[:, ic:ic + 1], in1=stmp[:],
                op0=ALU.mult, op1=ALU.mult),
                waits=[sl_ac])
            VE(lambda e, ic=ic: e.tensor_copy(out=agb[:, ic * TC:(ic + 1) * TC], in_=gtmp[:]))
            VE(lambda e, ic=ic: e.scalar_tensor_tensor(
                out=agb[:, 16384 + ic * TC:16384 + (ic + 1) * TC],
                in0=gtmp[:], scalar=1.0, in1=agb[:, ic * TC:(ic + 1) * TC],
                op0=ALU.mult, op1=ALU.subtract), inc=ve)
            ug_ev[ic] = ve.at()
        g_ve = ve.at()
        if taps and ti == 0:
            GD(lambda e: e.dma_start(out=tap["t_g"][:], in_=arena[:, A_KC:A_KC + 16384]),
               waits=[g_ve], inc=st)
        sa2_ev = {}
        for c in range(NIC):
            bank = c % 2
            w = [g_ve] if c == 0 else []
            if c >= 2:
                w = w + [sa2_ev[c - 2]]
            TE(lambda e, c=c, bank=bank: e.matmul(pp[bank][:, 0:TC], h128b_s[:],
                                                  agb[:, c * TC:(c + 1) * TC],
                                                  start=True, stop=False),
               waits=w)
            TE(lambda e, c=c, bank=bank: e.matmul(pp[bank][:, 0:TC], h128b_s[:],
                                                  agb[:, 16384 + c * TC:16384 + (c + 1) * TC],
                                                  start=False, stop=True),
               inc=mm)
            cm = mm.at()
            VE(lambda e, c=c, bank=bank: e.tensor_copy(
                out=aub[:, c * TC:(c + 1) * TC], in_=pp[bank][:, 0:TC]),
                waits=[cm])
            VE(lambda e, c=c, bank=bank: e.scalar_tensor_tensor(
                out=aub[:, 16384 + c * TC:16384 + (c + 1) * TC],
                in0=pp[bank][:, 0:TC], scalar=1.0, in1=aub[:, c * TC:(c + 1) * TC],
                op0=ALU.mult, op1=ALU.subtract), inc=ve)
            sa2_ev[c] = ve.at()
        VE(lambda e: e.drain(), inc=ve)
        u2_ve = ve.at()
        if taps and ti == 0:
            GD(lambda e: e.dma_start(out=tap["t_u2"][:], in_=arena[:, 0:16384]),
               waits=[u2_ve], inc=st)
        sb2_ev = {}
        CUSH("sync", [u2_ve])
        SY(lambda e: e.dma_start(out=u2d[:], in_=arena[:, 0:16384]), inc=ud)
        u2_ld = ud.at()
        g_gq64 = {}
        def em_gather64(q):
            fs = (q % 4) * 256
            w = [u2_ld] if q == 0 else []
            if q >= 4:
                w = w + [sb2_ev[q - 4]]
            SY(lambda e, q=q, fs=fs: e.dma_start(
                out=gb[:, fs:fs + 128],
                in_=u2d[2 * q:2 * q + 2, 0:8192].rearrange("s (c t) -> s c t", t=128)),
                waits=w, inc=gs[q % 4])
            SY(lambda e, q=q, fs=fs: e.dma_start(
                out=gb[:, fs + 128:fs + 256],
                in_=u2d[2 * q:2 * q + 2, 8192:16384].rearrange("s (c t) -> s c t", t=128)),
                inc=gs[q % 4])
            g_gq64[q] = gs[q % 4].at()
        for q in range(3):
            em_gather64(q)
        for q in range(NIC):
            slot = (q % 4) * 512
            if q + 3 < NIC:
                em_gather64(q + 3)
            wmm = [g_gq64[q]]
            if q >= 2:
                wmm = wmm + [sb2_ev[q - 2]]
            if q < 4:
                CUSH("tensor", [g_gq64[q]])
            TE(lambda e, q=q, slot=slot: e.matmul(pp[2 + q % 2][:, 0:TC], ih64b_s[:],
                                                  gbb[:, slot:slot + 256],
                                                  start=True, stop=False),
               waits=wmm)
            TE(lambda e, q=q, slot=slot: e.matmul(pp[2 + q % 2][:, 0:TC], ih64b_s[:],
                                                  gbb[:, slot + 256:slot + 512],
                                                  start=False, stop=True),
               inc=mm)
            qm = mm.at()
            VE(lambda e, q=q: e.tensor_copy(out=agb[:, q * TC:(q + 1) * TC],
                                            in_=pp[2 + q % 2][:, 0:TC]),
               waits=[qm])
            VE(lambda e, q=q: e.scalar_tensor_tensor(
                out=agb[:, 16384 + q * TC:16384 + (q + 1) * TC],
                in0=pp[2 + q % 2][:, 0:TC], scalar=1.0, in1=agb[:, q * TC:(q + 1) * TC],
                op0=ALU.mult, op1=ALU.subtract), inc=ve)
            sb2_ev[q] = ve.at()
        y2_ve = ve.at()
        if taps and ti == 0:
            GD(lambda e: e.dma_start(out=tap["t_y2"][:], in_=arena[:, A_KC:A_KC + 16384]),
               waits=[y2_ve], inc=st)
        wd_done = {}
        cache_ld = {}
        for q2 in range(NIC):
            par = q2 % 2
            boff = par * 4096
            w = [ug_ev[NIC - 1]] if q2 < 2 else [wd_done[q2 - 2]]
            if ti == 0:
                SY(lambda e, q2=q2: e.dma_start(out=wbuf[:, 4096:6144],
                                                in_=wdt[q2 * 128:(q2 + 1) * 128, :]),
                   waits=w + ([wd_done[q2 - 1]] if q2 >= 1 else []), inc=wf)
                wld = wf.at()
                VE(lambda e, boff=boff: e.tensor_copy(out=wbufb[:, boff:boff + 2048],
                                                      in_=wbuf[:, 4096:6144]),
                   waits=[wld] + ([cache_ld[q2 - 2]] if q2 >= 2 else []))
                VE(lambda e, boff=boff: e.scalar_tensor_tensor(
                    out=wbufb[:, boff + 2048:boff + 4096],
                    in0=wbuf[:, 4096:6144], scalar=1.0, in1=wbufb[:, boff:boff + 2048],
                    op0=ALU.mult, op1=ALU.subtract), inc=ve)
                wsp = ve.at()
            else:
                SY(lambda e, q2=q2, par=par: e.dma_start(
                    out=wbuf[:, par * 2048:par * 2048 + 2048], in_=wdb[q2][:]),
                   waits=w + ([wc_total] if ti == 1 and q2 < 2 else []), inc=ws[par])
                wsp = ws[par].at()
            for hc in range(16):
                wh = boff + hc * 128
                wl = boff + 2048 + hc * 128
                TE(lambda e, q2=q2, hc=hc, wh=wh: e.matmul(
                    pp[hc // 2][:, (hc % 2) * TC:(hc % 2 + 1) * TC],
                    wbufb[:, wh:wh + 128], agb[:, q2 * TC:(q2 + 1) * TC],
                    start=(q2 == 0 and hc % 2 == 0), stop=False,
                    skip_group_check=True),
                    waits=[wsp, y2_ve] if hc == 0 else ())
                TE(lambda e, q2=q2, hc=hc, wh=wh: e.matmul(
                    pp[hc // 2][:, (hc % 2) * TC:(hc % 2 + 1) * TC],
                    wbufb[:, wh:wh + 128], agb[:, 16384 + q2 * TC:16384 + (q2 + 1) * TC],
                    start=False, stop=False, skip_group_check=True))
                TE(lambda e, q2=q2, hc=hc, wl=wl: e.matmul(
                    pp[hc // 2][:, (hc % 2) * TC:(hc % 2 + 1) * TC],
                    wbufb[:, wl:wl + 128], agb[:, q2 * TC:(q2 + 1) * TC],
                    start=False, stop=(q2 == NIC - 1), skip_group_check=True),
                    inc=mm if hc == 15 else None)
            wd_done[q2] = mm.at()
            if ti == 0:
                SY(lambda e, q2=q2, par=par: e.dma_start(
                    out=wdb[q2][:], in_=wbuf[:, par * 2048:par * 2048 + 2048]),
                    waits=[wd_done[q2]], inc=wc)
                cache_ld[q2] = wc.at()
        dn_mm = mm.at()
        if ti == 0:
            wc_total = wc.at()
        for hc in range(16):
            VE(lambda e, hc=hc: e.scalar_tensor_tensor(
                out=arena[:, hc * TC:(hc + 1) * TC],
                in0=pp[hc // 2][:, (hc % 2) * TC:(hc % 2 + 1) * TC],
                scalar=wsd_s[:, hc:hc + 1], in1=xz[:, hc * TC:(hc + 1) * TC],
                op0=ALU.mult, op1=ALU.add),
                waits=[dn_mm] if hc == 0 else (), inc=ve if hc == 15 else None)
        out_ve = ve.at()
        if taps and ti == 0:
            GD(lambda e: e.dma_start(out=tap["t_out"][:], in_=arena[:, 0:4096]),
               waits=[out_ve], inc=st)
        CUSH("gpsimd", [out_ve])
        GD(lambda e, t0=t0: e.dma_start(
            out=y.rearrange("(c p) t -> p c t", p=128)[:, :, t0:t0 + TC],
            in_=arena[:, 0:4096].rearrange("p (c t) -> p c t", t=TC)),
            inc=st)
        prev_store = st.at()
        last_wq_done = wd_done[NIC - 1]

    final_st = st.at()
    P.add("gpsimd", lambda e: e.nop(), waits=[final_st])

    with nc.Block() as block:
        @block.sync
        def _(eng):
            P.replay("sync", eng)

        @block.tensor
        def _(eng):
            P.replay("tensor", eng)

        @block.vector
        def _(eng):
            P.replay("vector", eng)

        @block.scalar
        def _(eng):
            P.replay("scalar", eng)

        @block.gpsimd
        def _(eng):
            P.replay("gpsimd", eng)

    es.close()
    return nc


def _host_inputs(inputs):
    c = _host_consts()
    pug = _perm_ug()
    pdn = _perm_dn()
    wupt = np.ascontiguousarray(np.asarray(inputs["L0_ffn_W_up"]).T[pug, :])
    wgt = np.ascontiguousarray(np.asarray(inputs["L0_ffn_W_gate"]).T[pug, :])
    wdt = np.ascontiguousarray(np.asarray(inputs["L0_ffn_W_down"]).T[pdn, :])
    shared = {
        "wq": np.ascontiguousarray(np.asarray(inputs["L0_attn_Wq"])),
        "wk": np.ascontiguousarray(np.asarray(inputs["L0_attn_Wk"])),
        "wv": np.ascontiguousarray(np.asarray(inputs["L0_attn_Wv"])),
        "wo": np.ascontiguousarray(np.asarray(inputs["L0_attn_Wo"])),
        "wupt": wupt, "wgt": wgt, "wdt": wdt,
        "su_ug": np.ascontiguousarray(np.asarray(inputs["L0_ffn_SU_ug"])),
        "su_dp": np.ascontiguousarray(np.asarray(inputs["L0_ffn_SU_dp"])),
        "wsug": np.ascontiguousarray(np.asarray(inputs["L0_ffn_Wscale_ug"])),
        "wsdp": np.ascontiguousarray(np.asarray(inputs["L0_ffn_Wscale_dp"])),
        "cosT": c["cosT"], "sinF": c["sinF"],
        "h128": c["h128"], "ih16": c["ih16"], "ih64": c["ih64"],
        "ident": c["ident"], "mask": c["mask"], "onec": c["onec"], "oner": c["oner"],
        "h128b": c["h128"].astype(ml_dtypes.bfloat16),
        "ih64b": c["ih64"].astype(ml_dtypes.bfloat16),
    }
    x = np.asarray(inputs["L0_attn_x"])
    in_maps = []
    for i in range(8):
        m = dict(shared)
        m["xT"] = np.ascontiguousarray(x[i].T)
        in_maps.append(m)
    return in_maps


def kernel(**inputs):
    in_maps = _host_inputs(inputs)
    nc = build_program(taps=False)
    res = run_bass_kernel_spmd(nc, in_maps, core_ids=list(range(8)))
    out = np.stack([np.ascontiguousarray(r["y"].T) for r in res.results], axis=0)
    return out.astype(np.float32)


# version 05 (final)
# speedup vs baseline: 1.0684x; 1.0684x over previous
import numpy as np
import ml_dtypes
from contextlib import ExitStack

import concourse.bass as bass
import concourse.mybir as mybir
from concourse.bass_utils import run_bass_kernel_spmd

F32 = mybir.dt.float32
AF = mybir.ActivationFunctionType
ALU = mybir.AluOpType
AX = mybir.AxisListType

T = 2048
CD = 2048
NC = 16
H = 16
D = 128
INTER = 8192
NIC = 64
TC = 256
TB = 512
EPS = 1e-5
SCALE = float(1.0 / np.sqrt(np.float64(D)))
NTI = T // TC
NTB = T // TB


def _hadamard(n):
    h = np.array([[1.0]])
    while h.shape[0] < n:
        h = np.block([[h, h], [h, -h]])
    return h.astype(np.float32)


def _host_consts():
    c = {}
    c["h128"] = _hadamard(128)
    c["ih16"] = np.kron(np.eye(8), _hadamard(16)).astype(np.float32)
    c["ih64"] = np.kron(np.eye(2), _hadamard(64)).astype(np.float32)
    c["ident"] = np.eye(128, dtype=np.float32)
    c["h128b"] = c["h128"].astype(np.dtype("bfloat16") if False else np.float32)
    m = np.zeros((128, 128), dtype=np.float32)
    m[np.triu_indices(128, 1)] = -1e30
    c["mask"] = m
    c["onec"] = np.ones((128, 1), dtype=np.float32)
    c["oner"] = np.ones((1, 128), dtype=np.float32)
    inv = 1.0 / 10000.0 ** (np.arange(0, D, 2, dtype=np.float64) / D)
    f = np.arange(T, dtype=np.float64)[:, None] * inv[None, :]
    cosT = np.concatenate([np.cos(f)] * 2, axis=1).T.astype(np.float32)
    sinT = np.concatenate([np.sin(f)] * 2, axis=1).T.astype(np.float32)
    sgn = np.where(np.arange(128) < 64, -1.0, 1.0).astype(np.float32)[:, None]
    c["cosT"] = np.ascontiguousarray(cosT)
    c["sinF"] = np.ascontiguousarray(sinT * sgn)
    return c


def _perm_ug():
    p = np.empty(CD, dtype=np.int64)
    for q in range(16):
        for s in range(8):
            for ch in range(16):
                p[q * 128 + s * 16 + ch] = ch * 128 + 8 * q + s
    return p


def _perm_dn():
    p = np.empty(INTER, dtype=np.int64)
    for q in range(64):
        for s in range(2):
            for ch in range(64):
                p[q * 128 + s * 64 + ch] = ch * 128 + 2 * q + s
    return p


class Sem:
    def __init__(self, h):
        self.h = h
        self.n = 0

    def bump(self, k=1):
        self.n += k
        return (self.h, k)

    def at(self):
        return (self.h, self.n)


class Prog:
    def __init__(self):
        self.ops = []

    def add(self, eng, fn, waits=(), incs=()):
        self.ops.append((eng, fn, tuple(w for w in waits if w is not None), tuple(incs)))

    def replay(self, name, eng):
        for e, fn, waits, incs in self.ops:
            if e != name:
                continue
            for (h, v) in waits:
                eng.wait_ge(h, v)
            ins = fn(eng)
            for (h, k) in incs:
                ins.then_inc(h, k)


def build_program(taps=False):
    nc = bass.Bass("TRN2", target_bir_lowering=False, debug=False)

    dt = lambda n, s, k="ExternalInput": nc.dram_tensor(n, s, F32, kind=k).ap()
    xT = dt("xT", [CD, T])
    wq = dt("wq", [CD, CD])
    wk = dt("wk", [CD, 512])
    wv = dt("wv", [CD, 512])
    wo = dt("wo", [CD, CD])
    wupt = dt("wupt", [CD, INTER])
    wgt = dt("wgt", [CD, INTER])
    wdt = dt("wdt", [INTER, CD])
    su_ug = dt("su_ug", [CD])
    wsug = dt("wsug", [2 * INTER])
    su_dp = dt("su_dp", [INTER])
    wsdp = dt("wsdp", [CD])
    cosT = dt("cosT", [128, T])
    sinF = dt("sinF", [128, T])
    h128_d = dt("h128", [128, 128])
    ih16_d = dt("ih16", [128, 128])
    ih64_d = dt("ih64", [128, 128])
    ident_d = dt("ident", [128, 128])
    mask_d = dt("mask", [128, 128])
    onec_d = dt("onec", [128, 1])
    oner_d = dt("oner", [1, 128])
    y = dt("y", [CD, T], "ExternalOutput")
    ksp = dt("ksp", [4, 128, T], "Internal")
    u1d = dt("u1d", [128, 16 * TC], "Internal")
    u2d = dt("u2d", [128, NIC * TC], "Internal")
    wdb = dt("wdb", [NIC, 128, CD], "Internal")
    wqb = dt("wqb", [NC, 128, CD], "Internal")
    wob = dt("wob", [NC, 128, CD], "Internal")
    wub = dt("wub", [NIC, 128, CD], "Internal")
    wgb = dt("wgb", [NIC, 128, CD], "Internal")
    h128b = nc.dram_tensor("h128b", [128, 128], mybir.dt.bfloat16, kind="ExternalInput").ap()
    ih64b = nc.dram_tensor("ih64b", [128, 128], mybir.dt.bfloat16, kind="ExternalInput").ap()
    ih16b = nc.dram_tensor("ih16b", [128, 128], mybir.dt.bfloat16, kind="ExternalInput").ap()
    vsp = dt("vsp", [16, 128, 512], "Internal")
    tap = {}
    if taps:
        for nm, sh in [("t_k", [128, 4 * TB]), ("t_v", [128, 4 * TB]),
                       ("t_bz", [128, 16 * TB]), ("t_kraw", [128, 4 * TB]),
                       ("t_qpre", [128, NC * TC]), ("t_rs", [128, 16]),
                       ("t_praw", [128, 256]), ("t_s", [128, 256]),
                       ("t_gb", [128, 512]),
                       ("t_z", [128, NC * TC]), ("t_q", [128, NC * TC]),
                       ("t_p", [128, 256]), ("t_o", [128, NC * TC]),
                       ("t_xm", [128, NC * TC]), ("t_v2", [128, NC * TC]),
                       ("t_g", [128, NIC * TC]), ("t_u2", [128, NIC * TC]),
                       ("t_y2", [128, NIC * TC]), ("t_out", [128, NC * TC]),
                       ("t_stat", [1, 2048])]:
            tap[nm] = dt(nm, sh, "ExternalOutput")

    es = ExitStack()
    sb = lambda n, s: es.enter_context(nc.sbuf_tensor(n, s, F32)).ap()
    arena = sb("arena", [128, 32768])
    xz = sb("xz", [128, NC * TC])
    wbuf = sb("wbuf", [128, 8192])
    suhb = es.enter_context(nc.sbuf_tensor("suhb", [128, NC * 128], mybir.dt.bfloat16)).ap()
    ih16b_s = es.enter_context(nc.sbuf_tensor("ih16bs", [128, 128], mybir.dt.bfloat16)).ap()
    h128_s = sb("h128s", [128, 128])
    h128b_s = es.enter_context(nc.sbuf_tensor("h128bs", [128, 128], mybir.dt.bfloat16)).ap()
    ih64b_s = es.enter_context(nc.sbuf_tensor("ih64bs", [128, 128], mybir.dt.bfloat16)).ap()
    ih16_s = sb("ih16s", [128, 128])
    ih64_s = sb("ih64s", [128, 128])
    ident_s = sb("idents", [128, 128])
    mask_s = sb("masks", [128, 128])
    onec_s = sb("onecs", [128, 1])
    oner_s = sb("oners", [1, 128])
    suug_s = sb("suugs", [128, 16])
    sudp_s = sb("sudps", [128, 64])
    wsu_s = sb("wsus", [128, 64])
    wsg_s = sb("wsgs", [128, 64])
    wsd_s = sb("wsds", [128, 16])
    rows = sb("rows", [1, 2048])
    mb = sb("mb", [128, 512])
    rb = sb("rb", [128, 512])
    rstat = sb("rstat", [128, 16])
    ptb = sb("ptb", [128, 256])
    gb = sb("gb", [128, 1024])
    stmp = sb("stmp", [128, 256])
    gtmp = sb("gtmp", [128, 256])
    agb = arena[:, 16384:32768].bitcast(mybir.dt.bfloat16)
    aub = arena[:, 0:16384].bitcast(mybir.dt.bfloat16)
    gbb = gb.bitcast(mybir.dt.bfloat16)
    azb = arena[:, 0:4096].bitcast(mybir.dt.bfloat16)
    aob = arena[:, 8192:12288].bitcast(mybir.dt.bfloat16)
    aqb = arena[:, 4096:8192].bitcast(mybir.dt.bfloat16)
    wbufb = wbuf.bitcast(mybir.dt.bfloat16)

    ppA = es.enter_context(nc.psum_tensor("ppA", [128, 512], F32)).ap()
    ppS = es.enter_context(nc.psum_tensor("ppS", [128, 2048], F32)).ap()
    ppT = es.enter_context(nc.psum_tensor("ppT", [128, 512], F32)).ap()
    ppO = es.enter_context(nc.psum_tensor("ppO", [128, 512], F32)).ap()
    ppX = es.enter_context(nc.psum_tensor("ppX", [128, 512], F32)).ap()
    pp = [ppA, ppS[:, 0:512], ppS[:, 512:1024], ppS[:, 1024:1536],
          ppS[:, 1536:2048], ppT, ppO, ppX]

    ld = Sem(es.enter_context(nc.semaphore("ld")))
    gs = [Sem(es.enter_context(nc.semaphore(f"gs{i}"))) for i in range(4)]
    ws = [Sem(es.enter_context(nc.semaphore(f"ws{i}"))) for i in range(2)]
    cl = Sem(es.enter_context(nc.semaphore("cl")))
    kv = Sem(es.enter_context(nc.semaphore("kv")))
    sp = Sem(es.enter_context(nc.semaphore("sp")))
    ud = Sem(es.enter_context(nc.semaphore("ud")))
    wc = Sem(es.enter_context(nc.semaphore("wc")))
    wf = Sem(es.enter_context(nc.semaphore("wf")))
    gq = Sem(es.enter_context(nc.semaphore("gq")))
    st = Sem(es.enter_context(nc.semaphore("st")))
    mm = Sem(es.enter_context(nc.semaphore("mm")))
    ve = Sem(es.enter_context(nc.semaphore("ve")))
    ac = Sem(es.enter_context(nc.semaphore("ac")))

    P = Prog()

    def SY(fn, waits=(), inc=None):
        P.add("sync", fn, waits, (inc.bump(16),) if inc else ())

    def CUSH(eng, waits=()):
        P.add(eng, lambda e: e.nop(cycle_cnt=4000), waits)

    def GD(fn, waits=(), inc=None):
        P.add("gpsimd", fn, waits, (inc.bump(16),) if inc else ())

    def TE(fn, waits=(), inc=None):
        P.add("tensor", fn, waits, (inc.bump(1),) if inc else ())

    def VE(fn, waits=(), inc=None):
        P.add("vector", fn, waits, (inc.bump(1),) if inc else ())

    def AC(fn, waits=(), inc=None):
        P.add("scalar", fn, waits, (inc.bump(1),) if inc else ())

    for dst, src in [(h128_s, h128_d), (ih16_s, ih16_d), (ih64_s, ih64_d),
                     (h128b_s, h128b), (ih64b_s, ih64b), (ih16b_s, ih16b),
                     (ident_s, ident_d), (mask_s, mask_d), (onec_s, onec_d),
                     (oner_s, oner_d)]:
        SY(lambda e, d=dst, s=src: e.dma_start(out=d[:], in_=s[:]), inc=ld)
    SY(lambda e: e.dma_start(out=suug_s[:, 0:16],
                             in_=su_ug.rearrange("(c p) -> p c", p=128), allow_slow_non_contiguous=True), inc=ld)
    SY(lambda e: e.dma_start(out=sudp_s[:, 0:64],
                             in_=su_dp.rearrange("(c p) -> p c", p=128), allow_slow_non_contiguous=True), inc=ld)
    SY(lambda e: e.dma_start(out=wsu_s[:, 0:64],
                             in_=wsug[0:INTER].rearrange("(c p) -> p c", p=128), allow_slow_non_contiguous=True), inc=ld)
    SY(lambda e: e.dma_start(out=wsg_s[:, 0:64],
                             in_=wsug[INTER:2 * INTER].rearrange("(c p) -> p c", p=128), allow_slow_non_contiguous=True), inc=ld)
    SY(lambda e: e.dma_start(out=wsd_s[:, 0:16],
                             in_=wsdp.rearrange("(c p) -> p c", p=128), allow_slow_non_contiguous=True), inc=ld)
    setup_ld = ld.at()
    r2048 = float(1.0 / np.sqrt(np.float64(2048.0)))
    r8192 = float(1.0 / np.sqrt(np.float64(8192.0)))
    VE(lambda e: e.tensor_scalar(out=wsu_s[:, 0:64], in0=wsu_s[:, 0:64],
                                 scalar1=r2048, scalar2=None, op0=ALU.mult),
       waits=[setup_ld])
    VE(lambda e: e.tensor_mul(out=wsu_s[:, 0:64], in0=wsu_s[:, 0:64], in1=sudp_s[:, 0:64]))
    VE(lambda e: e.tensor_scalar(out=wsg_s[:, 0:64], in0=wsg_s[:, 0:64],
                                 scalar1=r2048, scalar2=None, op0=ALU.mult))
    VE(lambda e: e.tensor_scalar(out=wsd_s[:, 0:16], in0=wsd_s[:, 0:16],
                                 scalar1=r8192, scalar2=None, op0=ALU.mult))
    for c in range(NC):
        VE(lambda e, c=c: e.tensor_scalar(out=suhb[:, c * 128:(c + 1) * 128],
                                          in0=h128_s[:], scalar1=suug_s[:, c:c + 1],
                                          scalar2=None, op0=ALU.mult),
           inc=None)
    VE(lambda e: e.drain(), inc=ve)
    setup_ve = ve.at()

    A_Z = 0
    A_Q = 4096
    A_O = 8192
    A_SC = 12288
    A_KC = 16384
    A_VC = 24576

    def ln_stats(src, width, first_wait, sq0, sq1, sq_wait):
        for c in range(NC):
            TE(lambda e, c=c: e.matmul(pp[0][0:1, 0:width], onec_s[:],
                                       src[:, c * width:(c + 1) * width],
                                       start=(c == 0), stop=(c == NC - 1)),
               waits=first_wait if c == 0 else ())
        sqmm = {}
        for c in range(NC):
            sq = sq0 if c % 2 == 0 else sq1
            w = list(sq_wait) if c < 2 else [sqmm[c - 2]]
            AC(lambda e, sq=sq, c=c: e.activation(out=sq[:, 0:width],
                                                  in_=src[:, c * width:(c + 1) * width],
                                                  func=AF.Square),
               waits=w, inc=ac)
            aq = ac.at()
            TE(lambda e, sq=sq, c=c: e.matmul(pp[1][0:1, 0:width], onec_s[:], sq[:, 0:width],
                                              start=(c == 0), stop=(c == NC - 1)),
               waits=[aq], inc=mm)
            sqmm[c] = mm.at()
        allmm = mm.at()
        VE(lambda e: e.tensor_scalar(out=rows[0:1, 0:width], in0=pp[0][0:1, 0:width],
                                     scalar1=1.0 / CD, scalar2=None, op0=ALU.mult),
           waits=[allmm], inc=ve)
        VE(lambda e: e.tensor_scalar(out=rows[0:1, 512:512 + width], in0=pp[1][0:1, 0:width],
                                     scalar1=1.0 / CD, scalar2=None, op0=ALU.mult))
        VE(lambda e: e.tensor_mul(out=rows[0:1, 1024:1024 + width], in0=rows[0:1, 0:width],
                                  in1=rows[0:1, 0:width]))
        VE(lambda e: e.tensor_sub(out=rows[0:1, 512:512 + width], in0=rows[0:1, 512:512 + width],
                                  in1=rows[0:1, 1024:1024 + width]))
        VE(lambda e: e.tensor_scalar(out=rows[0:1, 512:512 + width], in0=rows[0:1, 512:512 + width],
                                     scalar1=EPS, scalar2=None, op0=ALU.add), inc=ve)
        var_ve = ve.at()
        AC(lambda e: e.activation(out=rows[0:1, 512:512 + width], in_=rows[0:1, 512:512 + width],
                                  func=AF.Sqrt), waits=[var_ve], inc=ac)
        sq_ac = ac.at()
        VE(lambda e: e.reciprocal(out=rows[0:1, 512:512 + width], in_=rows[0:1, 512:512 + width]),
           waits=[sq_ac], inc=ve)
        stats_ve = ve.at()
        TE(lambda e: e.matmul(pp[2][:, 0:width], oner_s[:], rows[0:1, 0:width],
                              start=True, stop=True), waits=[stats_ve], inc=mm)
        m1 = mm.at()
        VE(lambda e: e.tensor_copy(out=mb[:, 0:width], in_=pp[2][:, 0:width]),
           waits=[m1], inc=ve)
        bc1 = ve.at()
        TE(lambda e: e.matmul(pp[2][:, 0:width], oner_s[:], rows[0:1, 512:512 + width],
                              start=True, stop=True), waits=[bc1], inc=mm)
        m2 = mm.at()
        VE(lambda e: e.tensor_copy(out=rb[:, 0:width], in_=pp[2][:, 0:width]),
           waits=[m2], inc=ve)
        return ve.at()

    prev_spill = None
    for tib in range(NTB):
        t0 = tib * TB
        SY(lambda e, t0=t0: e.dma_start(out=arena[:, A_SC + 2048:A_SC + 2560],
                                        in_=cosT[:, t0:t0 + TB]),
           waits=[setup_ld], inc=cl)
        SY(lambda e, t0=t0: e.dma_start(out=arena[:, A_SC + 2560:A_SC + 3072],
                                        in_=sinF[:, t0:t0 + TB]), inc=cl)
        cs_ld = cl.at()
        SY(lambda e, t0=t0: e.dma_start(
            out=arena[:, 0:8192],
            in_=xT.rearrange("(c p) t -> p c t", p=128)[:, :, t0:t0 + TB]),
            waits=[prev_spill], inc=cl)
        x_ld = cl.at()
        bc_ve = ln_stats(arena[:, 0:8192], TB, [x_ld],
                         wbuf[:, 4096:4608], wbuf[:, 4608:5120], [x_ld])
        for c in range(NC):
            VE(lambda e, c=c: e.tensor_sub(out=arena[:, c * TB:(c + 1) * TB],
                                           in0=arena[:, c * TB:(c + 1) * TB], in1=mb[:, 0:TB]),
               waits=[bc_ve] if c == 0 else ())
            VE(lambda e, c=c: e.tensor_mul(out=arena[:, c * TB:(c + 1) * TB],
                                           in0=arena[:, c * TB:(c + 1) * TB], in1=rb[:, 0:TB]),
               inc=ve if c == NC - 1 else None)
        z_ve = ve.at()
        if taps and tib == 0:
            GD(lambda e: e.dma_start(out=tap["t_bz"][:], in_=arena[:, 0:8192]),
               waits=[z_ve], inc=st)
        wkv_done = {}
        for c in range(NC):
            off = 5120 + (c % 2) * 1024
            w = [wkv_done[c - 2]] if c >= 2 else [setup_ld]
            SY(lambda e, c=c, off=off: e.dma_start(out=wbuf[:, off:off + 512],
                                                   in_=wk[c * 128:(c + 1) * 128, :]),
               waits=w, inc=ws[c % 2])
            SY(lambda e, c=c, off=off: e.dma_start(out=wbuf[:, off + 512:off + 1024],
                                                   in_=wv[c * 128:(c + 1) * 128, :]), inc=ws[c % 2])
            wld = ws[c % 2].at()
            for kc in range(4):
                TE(lambda e, c=c, kc=kc, off=off: e.matmul(
                    pp[kc][:, 0:TB], wbuf[:, off + kc * 128:off + (kc + 1) * 128],
                    arena[:, c * TB:(c + 1) * TB],
                    start=(c == 0), stop=(c == NC - 1)),
                    waits=[wld, z_ve] if kc == 0 else ())
            for tb in range(4):
                TE(lambda e, c=c, tb=tb, off=off: e.matmul(
                    pp[4 + tb][:, 0:TB],
                    arena[:, c * TB + tb * 128:c * TB + (tb + 1) * 128],
                    wbuf[:, off + 512:off + 1024],
                    start=(c == 0), stop=(c == NC - 1)),
                    inc=mm if tb == 3 else None)
            wkv_done[c] = mm.at()
        kv_mm = mm.at()
        for kc in range(4):
            VE(lambda e, kc=kc: e.tensor_copy(out=arena[:, 8192 + kc * TB:8192 + (kc + 1) * TB],
                                              in_=pp[kc][:, 0:TB]),
               waits=[kv_mm] if kc == 0 else ())
        for tb in range(4):
            VE(lambda e, tb=tb: e.tensor_copy(out=arena[:, 10240 + tb * TB:10240 + (tb + 1) * TB],
                                              in_=pp[4 + tb][:, 0:TB]),
               inc=ve if tb == 3 else None)
        kv_ve = ve.at()
        if taps and tib == 0:
            GD(lambda e: e.dma_start(out=tap["t_kraw"][:], in_=arena[:, 8192:10240]),
               waits=[kv_ve], inc=st)
        CUSH("gpsimd", [kv_ve])
        GD(lambda e: e.dma_start(out=arena[0:64, A_SC:A_SC + 2048],
                                 in_=arena[64:128, 8192:10240]), inc=gq)
        GD(lambda e: e.dma_start(out=arena[64:128, A_SC:A_SC + 2048],
                                 in_=arena[0:64, 8192:10240]), inc=gq)
        rot_gq = gq.at()
        for kc in range(4):
            VE(lambda e, kc=kc: e.tensor_mul(out=arena[:, 8192 + kc * TB:8192 + (kc + 1) * TB],
                                             in0=arena[:, 8192 + kc * TB:8192 + (kc + 1) * TB],
                                             in1=arena[:, A_SC + 2048:A_SC + 2560]),
               waits=[rot_gq, cs_ld] if kc == 0 else ())
            VE(lambda e, kc=kc: e.tensor_mul(out=arena[:, A_SC + kc * TB:A_SC + (kc + 1) * TB],
                                             in0=arena[:, A_SC + kc * TB:A_SC + (kc + 1) * TB],
                                             in1=arena[:, A_SC + 2560:A_SC + 3072]))
            VE(lambda e, kc=kc: e.tensor_add(out=arena[:, 8192 + kc * TB:8192 + (kc + 1) * TB],
                                             in0=arena[:, 8192 + kc * TB:8192 + (kc + 1) * TB],
                                             in1=arena[:, A_SC + kc * TB:A_SC + (kc + 1) * TB]),
               inc=ve if kc == 3 else None)
        rope_ve = ve.at()
        CUSH("sync", [rope_ve])
        for kc in range(4):
            SY(lambda e, kc=kc, t0=t0: e.dma_start(out=ksp[kc][:, t0:t0 + TB],
                                                   in_=arena[:, 8192 + kc * TB:8192 + (kc + 1) * TB]),
               inc=sp)
        for tb in range(4):
            SY(lambda e, tb=tb, tib=tib: e.dma_start(out=vsp[tib * 4 + tb][:],
                                                     in_=arena[:, 10240 + tb * TB:10240 + (tb + 1) * TB]),
               inc=sp)
        prev_spill = sp.at()
        if taps and tib == 0:
            GD(lambda e: e.dma_start(out=tap["t_k"][:], in_=arena[:, 8192:10240]),
               waits=[rope_ve], inc=st)
            GD(lambda e: e.dma_start(out=tap["t_v"][:], in_=arena[:, 10240:12288]),
               waits=[kv_ve], inc=st)
    spills_done = prev_spill

    prev_store = None
    last_wq_done = None
    wc_total = None
    cacheq_total = None
    cacheo_total = None
    cacheu_total = None
    cache_q = {}
    cache_o = {}
    cache_u = {}
    for ti in range(NTI):
        t0 = ti * TC
        nkk = t0 + TC
        nkb = nkk // 128
        SY(lambda e, t0=t0: e.dma_start(out=gb[:, 0:TC], in_=cosT[:, t0:t0 + TC]),
           waits=[spills_done, prev_store], inc=cl)
        SY(lambda e, t0=t0: e.dma_start(out=gb[:, TC:2 * TC], in_=sinF[:, t0:t0 + TC]),
           inc=cl)
        cs_ld = cl.at()
        SY(lambda e, t0=t0: e.dma_start(
            out=xz[:], in_=xT.rearrange("(c p) t -> p c t", p=128)[:, :, t0:t0 + TC]),
            inc=cl)
        x_ld = cl.at()
        for kc in range(4):
            SY(lambda e, kc=kc, nkk=nkk: e.dma_start(
                out=arena[:, A_KC + kc * 2048:A_KC + kc * 2048 + nkk],
                in_=ksp[kc][:, 0:nkk]), inc=kv)
        for g in range(nkb):
            SY(lambda e, g=g: e.dma_start(
                out=arena[:, A_VC + g * 512:A_VC + (g + 1) * 512], in_=vsp[g][:]), inc=kv)
        kv_ld = kv.at()
        bc_ve = ln_stats(xz, TC, [x_ld], stmp, ptb, [x_ld])
        for c in range(NC):
            VE(lambda e, c=c: e.tensor_sub(out=stmp[:], in0=xz[:, c * TC:(c + 1) * TC],
                                           in1=mb[:, 0:TC]),
               waits=[bc_ve] if c == 0 else ())
            VE(lambda e: e.tensor_mul(out=stmp[:], in0=stmp[:], in1=rb[:, 0:TC]))
            VE(lambda e, c=c: e.tensor_copy(out=azb[:, c * TC:(c + 1) * TC], in_=stmp[:]))
            VE(lambda e, c=c: e.scalar_tensor_tensor(
                out=azb[:, 4096 + c * TC:4096 + (c + 1) * TC],
                in0=stmp[:], scalar=1.0, in1=azb[:, c * TC:(c + 1) * TC],
                op0=ALU.mult, op1=ALU.subtract),
                inc=ve if c == NC - 1 else None)
        z_ve = ve.at()
        if taps and ti == 0:
            GD(lambda e: e.dma_start(out=tap["t_z"][:], in_=arena[:, A_Z:A_Z + 4096]),
               waits=[z_ve], inc=st)
            GD(lambda e: e.dma_start(out=tap["t_stat"][0:1, :], in_=rows[0:1, :]),
               waits=[z_ve], inc=st)
        wq_done = {}
        for c in range(NC):
            par = c % 2 if ti > 0 else 0
            bbase = par * 4096
            if ti == 0:
                w = [wq_done[c - 1]] if c >= 1 else [last_wq_done]
                SY(lambda e, c=c: e.dma_start(out=wbuf[:, 4096:6144],
                                              in_=wq[c * 128:(c + 1) * 128, :]),
                   waits=w, inc=wf)
                wld = wf.at()
                VE(lambda e, bbase=bbase: e.tensor_copy(out=wbufb[:, bbase:bbase + 2048],
                                                        in_=wbuf[:, 4096:6144]),
                   waits=[wld] + ([cache_q[c - 1]] if c >= 1 else []))
                VE(lambda e, bbase=bbase: e.scalar_tensor_tensor(
                    out=wbufb[:, bbase + 2048:bbase + 4096],
                    in0=wbuf[:, 4096:6144], scalar=1.0, in1=wbufb[:, bbase:bbase + 2048],
                    op0=ALU.mult, op1=ALU.subtract), inc=ve)
                wsp = ve.at()
            else:
                w = [wq_done[c - 2]] if c >= 2 else [last_wq_done]
                SY(lambda e, c=c, par=par: e.dma_start(
                    out=wbuf[:, par * 2048:par * 2048 + 2048], in_=wqb[c][:]),
                   waits=w + ([cacheq_total] if ti == 1 and c < 2 else []), inc=ws[par])
                wsp = ws[par].at()
            for qc in range(16):
                wh = bbase + qc * 128
                wl = bbase + 2048 + qc * 128
                ww = [wsp, z_ve] if qc == 0 else ()
                TE(lambda e, c=c, qc=qc, wh=wh: e.matmul(
                    pp[qc // 2][:, (qc % 2) * TC:(qc % 2 + 1) * TC],
                    wbufb[:, wh:wh + 128], azb[:, c * TC:(c + 1) * TC],
                    start=(c == 0 and qc % 2 == 0), stop=False,
                    skip_group_check=True), waits=ww)
                TE(lambda e, c=c, qc=qc, wh=wh: e.matmul(
                    pp[qc // 2][:, (qc % 2) * TC:(qc % 2 + 1) * TC],
                    wbufb[:, wh:wh + 128], azb[:, 4096 + c * TC:4096 + (c + 1) * TC],
                    start=False, stop=False, skip_group_check=True))
                TE(lambda e, c=c, qc=qc, wl=wl: e.matmul(
                    pp[qc // 2][:, (qc % 2) * TC:(qc % 2 + 1) * TC],
                    wbufb[:, wl:wl + 128], azb[:, c * TC:(c + 1) * TC],
                    start=False, stop=(c == NC - 1), skip_group_check=True),
                    inc=mm if qc == 15 else None)
            wq_done[c] = mm.at()
            if ti == 0:
                SY(lambda e, c=c, bbase=bbase: e.dma_start(
                    out=wqb[c][:], in_=wbuf[:, bbase // 2:bbase // 2 + 2048]),
                    waits=[wq_done[c]], inc=wc)
                cache_q[c] = wc.at()
        if ti == 0:
            cacheq_total = wc.at()
        q_mm = mm.at()
        for qc in range(16):
            VE(lambda e, qc=qc: e.tensor_copy(
                out=arena[:, A_Q + qc * TC:A_Q + (qc + 1) * TC],
                in_=pp[qc // 2][:, (qc % 2) * TC:(qc % 2 + 1) * TC]),
                waits=[q_mm] if qc == 0 else (), inc=ve if qc == 15 else None)
        qe_ve = ve.at()
        if taps and ti == 0:
            GD(lambda e: e.dma_start(out=tap["t_qpre"][:], in_=arena[:, A_Q:A_Q + 4096]),
               waits=[qe_ve], inc=st)
        CUSH("gpsimd", [qe_ve, cs_ld])
        GD(lambda e: e.dma_start(out=arena[0:64, A_SC:A_SC + 4096],
                                 in_=arena[64:128, A_Q:A_Q + 4096]), inc=gq)
        GD(lambda e: e.dma_start(out=arena[64:128, A_SC:A_SC + 4096],
                                 in_=arena[0:64, A_Q:A_Q + 4096]), inc=gq)
        rot_gq = gq.at()
        VE(lambda e: e.tensor_scalar(out=gb[:, 0:TC], in0=gb[:, 0:TC],
                                     scalar1=SCALE, scalar2=None, op0=ALU.mult),
           waits=[cs_ld])
        VE(lambda e: e.tensor_scalar(out=gb[:, TC:2 * TC], in0=gb[:, TC:2 * TC],
                                     scalar1=SCALE, scalar2=None, op0=ALU.mult))
        for qc in range(16):
            VE(lambda e, qc=qc: e.tensor_mul(out=arena[:, A_Q + qc * TC:A_Q + (qc + 1) * TC],
                                             in0=arena[:, A_Q + qc * TC:A_Q + (qc + 1) * TC],
                                             in1=gb[:, 0:TC]),
               waits=[rot_gq] if qc == 0 else ())
            VE(lambda e, qc=qc: e.tensor_mul(out=arena[:, A_SC + qc * TC:A_SC + (qc + 1) * TC],
                                             in0=arena[:, A_SC + qc * TC:A_SC + (qc + 1) * TC],
                                             in1=gb[:, TC:2 * TC]))
            VE(lambda e, qc=qc: e.tensor_add(out=arena[:, A_Q + qc * TC:A_Q + (qc + 1) * TC],
                                             in0=arena[:, A_Q + qc * TC:A_Q + (qc + 1) * TC],
                                             in1=arena[:, A_SC + qc * TC:A_SC + (qc + 1) * TC]),
               inc=ve if qc == 15 else None)
        rope_ve = ve.at()
        if taps and ti == 0:
            GD(lambda e: e.dma_start(out=tap["t_q"][:], in_=arena[:, A_Q:A_Q + 4096]),
               waits=[rope_ve], inc=st)
        p_off = [A_SC, A_SC + 2048]
        prev_exp = None
        prev_tp_mm = None
        o_prev = None
        for h in range(H):
            kh = h // 4
            for qb in range(2):
                qi = ti * 2 + qb
                nkq = (qi + 1) * 128
                nj = (nkq + 511) // 512
                for j in range(nj):
                    wd_ = min(512, nkq - j * 512)
                    w = []
                    if h == 0 and qb == 0 and j == 0:
                        w = [rope_ve, kv_ld]
                    if prev_exp is not None and j == 0:
                        w = w + [prev_exp]
                    TE(lambda e, h=h, qb=qb, j=j, wd_=wd_, kh=kh: e.matmul(
                        pp[1 + j][:, 0:wd_],
                        arena[:, A_Q + h * TC + qb * 128:A_Q + h * TC + qb * 128 + 128],
                        arena[:, A_KC + kh * 2048 + j * 512:A_KC + kh * 2048 + j * 512 + wd_],
                        start=True, stop=True),
                        waits=w, inc=mm if j == nj - 1 else None)
                s_mm = mm.at()
                if taps and ti == 0 and h == 0 and qb == 0:
                    VE(lambda e: e.tensor_copy(out=stmp[:, 0:128], in_=ppS[:, 0:128]),
                       waits=[s_mm], inc=ve)
                    GD(lambda e: e.dma_start(out=tap["t_s"][:, 0:128], in_=stmp[:, 0:128]),
                       waits=[ve.at()], inc=st)
                jd = (nkq - 128) // 512
                od = (nkq - 128) % 512
                VE(lambda e, jd=jd, od=od: e.tensor_add(out=pp[1 + jd][:, od:od + 128],
                                                        in0=pp[1 + jd][:, od:od + 128],
                                                        in1=mask_s[:]),
                   waits=[s_mm] + ([prev_exp] if prev_exp else []))
                VE(lambda e, nkq=nkq: e.reduce_max(rstat[:, 0:1], ppS[:, 0:nkq],
                                                   axis=AX.X, negate=True), inc=ve)
                mx_ve = ve.at()
                w = [mx_ve]
                if prev_tp_mm is not None:
                    w = w + [prev_tp_mm]
                AC(lambda e, nkq=nkq, qb=qb: e.activation(
                    out=arena[:, p_off[qb]:p_off[qb] + nkq],
                    in_=ppS[:, 0:nkq], func=AF.Exp,
                    bias=rstat[:, 0:1], scale=1.0,
                    accum_out=rstat[:, 1:2]),
                    waits=w, inc=ac)
                prev_exp = ac.at()
                tap_w = []
                if taps and ti == 0 and h == 0 and qb == 0:
                    GD(lambda e: e.dma_start(out=tap["t_praw"][:, 0:256],
                                             in_=arena[:, A_SC:A_SC + 256]),
                       waits=[prev_exp], inc=st)
                    tap_w = [st.at()]
                VE(lambda e: e.reciprocal(out=rstat[:, 2:3], in_=rstat[:, 1:2]),
                   waits=[prev_exp] + tap_w)
                VE(lambda e: e.drain())
                VE(lambda e, qb=qb, nkq=nkq: e.tensor_scalar(
                    out=arena[:, p_off[qb]:p_off[qb] + nkq],
                    in0=arena[:, p_off[qb]:p_off[qb] + nkq],
                    scalar1=rstat[:, 2:3], scalar2=None, op0=ALU.mult), inc=ve)
                if taps and ti == 0 and h == 0 and qb == 0:
                    GD(lambda e: e.dma_start(out=tap["t_rs"][:], in_=rstat[:]),
                       waits=[ve.at()], inc=st)
            p_ve = ve.at()
            if taps and ti == 0 and h == 0:
                GD(lambda e: e.dma_start(out=tap["t_p"][:, 0:256],
                                         in_=arena[:, A_SC:A_SC + 256]),
                   waits=[p_ve], inc=st)
            nb0 = (ti * 2 + 1)
            pt_ev = None
            for kb in range(nkb):
                both = kb < nb0
                w = [p_ve] if kb == 0 else []
                if pt_ev is not None:
                    w = w + [pt_ev]
                TE(lambda e, kb=kb: e.transpose(
                    pp[5][:, 128:256],
                    arena[:, A_SC + 2048 + kb * 128:A_SC + 2048 + (kb + 1) * 128],
                    ident_s[:]), waits=w, inc=None if both else mm)
                if both:
                    TE(lambda e, kb=kb: e.transpose(
                        pp[5][:, 0:128],
                        arena[:, A_SC + kb * 128:A_SC + (kb + 1) * 128],
                        ident_s[:]), inc=mm)
                t_mm = mm.at()
                VE(lambda e, both=both: e.tensor_copy(
                    out=ptb[:, 0:256] if both else ptb[:, 128:256],
                    in_=pp[5][:, 0:256] if both else pp[5][:, 128:256]),
                    waits=[t_mm], inc=ve)
                pt_ev = ve.at()
                w2 = [pt_ev]
                if kb == 0 and o_prev is not None:
                    w2 = w2 + [o_prev]
                if both:
                    TE(lambda e, kb=kb, kh=kh: e.matmul(
                        pp[6][:, 0:256],
                        arena[:, A_VC + kb * 512 + kh * 128:A_VC + kb * 512 + (kh + 1) * 128],
                        ptb[:, 0:256], start=(kb == 0), stop=(kb == nkb - 1)),
                        waits=w2, inc=mm)
                else:
                    TE(lambda e, kb=kb, kh=kh: e.matmul(
                        pp[6][:, 128:256],
                        arena[:, A_VC + kb * 512 + kh * 128:A_VC + kb * 512 + (kh + 1) * 128],
                        ptb[:, 128:256], start=False, stop=(kb == nkb - 1),
                        skip_group_check=True),
                        waits=w2, inc=mm)
            pv_mm = mm.at()
            prev_tp_mm = pv_mm
            VE(lambda e, h=h: e.tensor_copy(out=aob[:, h * TC:(h + 1) * TC],
                                            in_=pp[6][:, 0:256]),
               waits=[pv_mm])
            VE(lambda e, h=h: e.scalar_tensor_tensor(
                out=aob[:, 4096 + h * TC:4096 + (h + 1) * TC],
                in0=pp[6][:, 0:256], scalar=1.0, in1=aob[:, h * TC:(h + 1) * TC],
                op0=ALU.mult, op1=ALU.subtract), inc=ve)
            o_prev = ve.at()
        attn_ve = o_prev
        if taps and ti == 0:
            GD(lambda e: e.dma_start(out=tap["t_o"][:], in_=arena[:, A_O:A_O + 4096]),
               waits=[attn_ve], inc=st)
        wo_done = {}
        for c in range(NC):
            par = c % 2 if ti > 0 else 0
            bbase = par * 4096
            if ti == 0:
                w = [wo_done[c - 1]] if c >= 1 else [wq_done[NC - 1]]
                SY(lambda e, c=c: e.dma_start(out=wbuf[:, 4096:6144],
                                              in_=wo[c * 128:(c + 1) * 128, :]),
                   waits=w, inc=wf)
                wld = wf.at()
                VE(lambda e, bbase=bbase: e.tensor_copy(out=wbufb[:, bbase:bbase + 2048],
                                                        in_=wbuf[:, 4096:6144]),
                   waits=[wld] + ([cache_o[c - 1]] if c >= 1 else []))
                VE(lambda e, bbase=bbase: e.scalar_tensor_tensor(
                    out=wbufb[:, bbase + 2048:bbase + 4096],
                    in0=wbuf[:, 4096:6144], scalar=1.0, in1=wbufb[:, bbase:bbase + 2048],
                    op0=ALU.mult, op1=ALU.subtract), inc=ve)
                wsp = ve.at()
            else:
                w = [wo_done[c - 2]] if c >= 2 else [wq_done[NC - 1]]
                SY(lambda e, c=c, par=par: e.dma_start(
                    out=wbuf[:, par * 2048:par * 2048 + 2048], in_=wob[c][:]),
                   waits=w + ([cacheo_total] if ti == 1 and c < 2 else []), inc=ws[par])
                wsp = ws[par].at()
            for oc in range(16):
                wh = bbase + oc * 128
                wl = bbase + 2048 + oc * 128
                ww = [wsp, attn_ve] if oc == 0 else ()
                TE(lambda e, c=c, oc=oc, wh=wh: e.matmul(
                    pp[oc // 2][:, (oc % 2) * TC:(oc % 2 + 1) * TC],
                    wbufb[:, wh:wh + 128], aob[:, c * TC:(c + 1) * TC],
                    start=(c == 0 and oc % 2 == 0), stop=False,
                    skip_group_check=True), waits=ww)
                TE(lambda e, c=c, oc=oc, wh=wh: e.matmul(
                    pp[oc // 2][:, (oc % 2) * TC:(oc % 2 + 1) * TC],
                    wbufb[:, wh:wh + 128], aob[:, 4096 + c * TC:4096 + (c + 1) * TC],
                    start=False, stop=False, skip_group_check=True))
                TE(lambda e, c=c, oc=oc, wl=wl: e.matmul(
                    pp[oc // 2][:, (oc % 2) * TC:(oc % 2 + 1) * TC],
                    wbufb[:, wl:wl + 128], aob[:, c * TC:(c + 1) * TC],
                    start=False, stop=(c == NC - 1), skip_group_check=True),
                    inc=mm if oc == 15 else None)
            wo_done[c] = mm.at()
            if ti == 0:
                SY(lambda e, c=c, bbase=bbase: e.dma_start(
                    out=wob[c][:], in_=wbuf[:, bbase // 2:bbase // 2 + 2048]),
                    waits=[wo_done[c]], inc=wc)
                cache_o[c] = wc.at()
        if ti == 0:
            cacheo_total = wc.at()
        wo_mm = mm.at()
        for oc in range(16):
            VE(lambda e, oc=oc: e.tensor_add(out=xz[:, oc * TC:(oc + 1) * TC],
                                             in0=pp[oc // 2][:, (oc % 2) * TC:(oc % 2 + 1) * TC],
                                             in1=xz[:, oc * TC:(oc + 1) * TC]),
               waits=[wo_mm] if oc == 0 else (), inc=ve if oc == 15 else None)
        xm_ve = ve.at()
        if taps and ti == 0:
            GD(lambda e: e.dma_start(out=tap["t_xm"][:], in_=xz[:]),
               waits=[xm_ve], inc=st)

        bc2_ve = ln_stats(xz, TC, [xm_ve], stmp, ptb, [xm_ve])
        for c in range(NC):
            VE(lambda e, c=c: e.tensor_sub(out=stmp[:], in0=xz[:, c * TC:(c + 1) * TC],
                                           in1=mb[:, 0:TC]),
               waits=[bc2_ve] if c == 0 else ())
            VE(lambda e: e.tensor_mul(out=stmp[:], in0=stmp[:], in1=rb[:, 0:TC]))
            VE(lambda e, c=c: e.tensor_copy(out=azb[:, c * TC:(c + 1) * TC], in_=stmp[:]))
            VE(lambda e, c=c: e.scalar_tensor_tensor(
                out=azb[:, 4096 + c * TC:4096 + (c + 1) * TC],
                in0=stmp[:], scalar=1.0, in1=azb[:, c * TC:(c + 1) * TC],
                op0=ALU.mult, op1=ALU.subtract),
                inc=ve if c == NC - 1 else None)
        z2_ve = ve.at()
        sa_ev = {}
        for c in range(NC):
            bank = c % 2
            w = [z2_ve] if c == 0 else []
            if c >= 2:
                w = w + [sa_ev[c - 2]]
            TE(lambda e, c=c, bank=bank: e.matmul(pp[bank][:, 0:TC],
                                                  suhb[:, c * 128:(c + 1) * 128],
                                                  azb[:, c * TC:(c + 1) * TC],
                                                  start=True, stop=False),
               waits=w)
            TE(lambda e, c=c, bank=bank: e.matmul(pp[bank][:, 0:TC],
                                                  suhb[:, c * 128:(c + 1) * 128],
                                                  azb[:, 4096 + c * TC:4096 + (c + 1) * TC],
                                                  start=False, stop=True),
               inc=mm)
            cm = mm.at()
            VE(lambda e, c=c, bank=bank: e.tensor_copy(
                out=aqb[:, c * TC:(c + 1) * TC], in_=pp[bank][:, 0:TC]),
                waits=[cm])
            VE(lambda e, c=c, bank=bank: e.scalar_tensor_tensor(
                out=aqb[:, 4096 + c * TC:4096 + (c + 1) * TC],
                in0=pp[bank][:, 0:TC], scalar=1.0, in1=aqb[:, c * TC:(c + 1) * TC],
                op0=ALU.mult, op1=ALU.subtract), inc=ve)
            sa_ev[c] = ve.at()
        VE(lambda e: e.drain(), inc=ve)
        u1_ve = ve.at()
        sb_ev = {}
        CUSH("sync", [u1_ve])
        SY(lambda e: e.dma_start(out=u1d[:], in_=arena[:, A_Q:A_Q + 4096]), inc=ud)
        u1_ld = ud.at()
        g_gq16 = {}
        def em_gather16(q):
            fs = (q % 4) * 256
            w = [u1_ld] if q == 0 else []
            if q >= 4:
                w = w + [sb_ev[q - 4]]
            SY(lambda e, q=q, fs=fs: e.dma_start(
                out=gb[:, fs:fs + 128],
                in_=u1d[8 * q:8 * q + 8, 0:2048].rearrange("s (c t) -> s c t", t=128)),
                waits=w, inc=gs[q % 4])
            SY(lambda e, q=q, fs=fs: e.dma_start(
                out=gb[:, fs + 128:fs + 256],
                in_=u1d[8 * q:8 * q + 8, 2048:4096].rearrange("s (c t) -> s c t", t=128)),
                inc=gs[q % 4])
            g_gq16[q] = gs[q % 4].at()
        for q in range(3):
            em_gather16(q)
        for q in range(16):
            slot = (q % 4) * 512
            if q + 3 < 16:
                em_gather16(q + 3)
            wmm = [g_gq16[q]]
            if q >= 2:
                wmm = wmm + [sb_ev[q - 2]]
            if q < 4:
                CUSH("tensor", [g_gq16[q]])
            TE(lambda e, q=q, slot=slot: e.matmul(pp[2 + q % 2][:, 0:TC], ih16b_s[:],
                                                  gbb[:, slot:slot + 256],
                                                  start=True, stop=False),
               waits=wmm)
            TE(lambda e, q=q, slot=slot: e.matmul(pp[2 + q % 2][:, 0:TC], ih16b_s[:],
                                                  gbb[:, slot + 256:slot + 512],
                                                  start=False, stop=True),
               inc=mm)
            qm = mm.at()
            VE(lambda e, q=q: e.tensor_copy(out=aob[:, q * TC:(q + 1) * TC],
                                            in_=pp[2 + q % 2][:, 0:TC]),
               waits=[qm])
            VE(lambda e, q=q: e.scalar_tensor_tensor(
                out=aob[:, 4096 + q * TC:4096 + (q + 1) * TC],
                in0=pp[2 + q % 2][:, 0:TC], scalar=1.0, in1=aob[:, q * TC:(q + 1) * TC],
                op0=ALU.mult, op1=ALU.subtract), inc=ve)
            sb_ev[q] = ve.at()
        v2_ve = ve.at()
        if taps and ti == 0:
            GD(lambda e: e.dma_start(out=tap["t_v2"][:], in_=arena[:, A_O:A_O + 4096]),
               waits=[v2_ve], inc=st)
        ug_ev = {}
        for ic in range(NIC):
            if ti == 0:
                ubase, gbase = 0, 4096
                w = [wo_done[NC - 1]] if ic == 0 else [ug_ev[ic - 1]]
                SY(lambda e, ic=ic: e.dma_start(
                    out=wbuf[:, 4096:6144],
                    in_=wupt.rearrange("(c p) m -> p c m", p=128)[:, :, ic * 128:(ic + 1) * 128]),
                    waits=w, inc=wf)
                SY(lambda e, ic=ic: e.dma_start(
                    out=wbuf[:, 6144:8192],
                    in_=wgt.rearrange("(c p) m -> p c m", p=128)[:, :, ic * 128:(ic + 1) * 128]),
                    inc=wf)
                wld = wf.at()
                VE(lambda e: e.tensor_copy(out=wbufb[:, 0:2048], in_=wbuf[:, 4096:6144]),
                   waits=[wld] + ([cache_u[ic - 1]] if ic >= 1 else []))
                VE(lambda e: e.scalar_tensor_tensor(
                    out=wbufb[:, 2048:4096], in0=wbuf[:, 4096:6144], scalar=1.0,
                    in1=wbufb[:, 0:2048], op0=ALU.mult, op1=ALU.subtract))
                VE(lambda e: e.tensor_copy(out=wbufb[:, 4096:6144], in_=wbuf[:, 6144:8192]))
                VE(lambda e: e.scalar_tensor_tensor(
                    out=wbufb[:, 6144:8192], in0=wbuf[:, 6144:8192], scalar=1.0,
                    in1=wbufb[:, 4096:6144], op0=ALU.mult, op1=ALU.subtract), inc=ve)
                wsp = ve.at()
            else:
                par = ic % 2
                ubase, gbase = par * 8192, par * 8192 + 4096
                w = [wo_done[NC - 1]] if ic < 2 else [ug_ev[ic - 2]]
                SY(lambda e, ic=ic, par=par: e.dma_start(
                    out=wbuf[:, par * 4096:par * 4096 + 2048], in_=wub[ic][:]),
                    waits=w + ([cacheu_total] if ti == 1 and ic < 2 else []), inc=ws[par])
                SY(lambda e, ic=ic, par=par: e.dma_start(
                    out=wbuf[:, par * 4096 + 2048:par * 4096 + 4096], in_=wgb[ic][:]),
                    inc=ws[par])
                wsp = ws[par].at()
            bu, bg = (4, 5) if ic % 2 == 0 else (6, 7)
            wmm = [v2_ve, wsp]
            if ic >= 2:
                wmm = wmm + [ug_ev[ic - 2]]
            for c in range(NC):
                uh = ubase + c * 128
                ul = ubase + 2048 + c * 128
                gh = gbase + c * 128
                gl = gbase + 2048 + c * 128
                TE(lambda e, c=c, uh=uh, bu=bu: e.matmul(
                    pp[bu][:, 0:TC], wbufb[:, uh:uh + 128],
                    aob[:, c * TC:(c + 1) * TC],
                    start=(c == 0), stop=False),
                    waits=wmm if c == 0 else ())
                TE(lambda e, c=c, uh=uh, bu=bu: e.matmul(
                    pp[bu][:, 0:TC], wbufb[:, uh:uh + 128],
                    aob[:, 4096 + c * TC:4096 + (c + 1) * TC],
                    start=False, stop=False))
                TE(lambda e, c=c, ul=ul, bu=bu: e.matmul(
                    pp[bu][:, 0:TC], wbufb[:, ul:ul + 128],
                    aob[:, c * TC:(c + 1) * TC],
                    start=False, stop=(c == NC - 1)))
                TE(lambda e, c=c, gh=gh, bg=bg: e.matmul(
                    pp[bg][:, 0:TC], wbufb[:, gh:gh + 128],
                    aob[:, c * TC:(c + 1) * TC],
                    start=(c == 0), stop=False))
                TE(lambda e, c=c, gh=gh, bg=bg: e.matmul(
                    pp[bg][:, 0:TC], wbufb[:, gh:gh + 128],
                    aob[:, 4096 + c * TC:4096 + (c + 1) * TC],
                    start=False, stop=False))
                TE(lambda e, c=c, gl=gl, bg=bg: e.matmul(
                    pp[bg][:, 0:TC], wbufb[:, gl:gl + 128],
                    aob[:, c * TC:(c + 1) * TC],
                    start=False, stop=(c == NC - 1)),
                    inc=mm if c == NC - 1 else None)
            ug_mm = mm.at()
            if ti == 0:
                SY(lambda e, ic=ic: e.dma_start(out=wub[ic][:], in_=wbuf[:, 0:2048]),
                   waits=[ug_mm], inc=wc)
                SY(lambda e, ic=ic: e.dma_start(out=wgb[ic][:], in_=wbuf[:, 2048:4096]),
                   inc=wc)
                cache_u[ic] = wc.at()
            AC(lambda e, ic=ic, bg=bg: e.activation(out=stmp[:], in_=pp[bg][:, 0:TC],
                                                    func=AF.Silu, scale=wsg_s[:, ic:ic + 1]),
               waits=[ug_mm], inc=ac)
            sl_ac = ac.at()
            VE(lambda e, ic=ic, bu=bu: e.scalar_tensor_tensor(
                out=gtmp[:], in0=pp[bu][:, 0:TC], scalar=wsu_s[:, ic:ic + 1], in1=stmp[:],
                op0=ALU.mult, op1=ALU.mult),
                waits=[sl_ac])
            VE(lambda e, ic=ic: e.tensor_copy(out=agb[:, ic * TC:(ic + 1) * TC], in_=gtmp[:]))
            VE(lambda e, ic=ic: e.scalar_tensor_tensor(
                out=agb[:, 16384 + ic * TC:16384 + (ic + 1) * TC],
                in0=gtmp[:], scalar=1.0, in1=agb[:, ic * TC:(ic + 1) * TC],
                op0=ALU.mult, op1=ALU.subtract), inc=ve)
            ug_ev[ic] = ve.at()
        if ti == 0:
            cacheu_total = wc.at()
        g_ve = ve.at()
        if taps and ti == 0:
            GD(lambda e: e.dma_start(out=tap["t_g"][:], in_=arena[:, A_KC:A_KC + 16384]),
               waits=[g_ve], inc=st)
        sa2_ev = {}
        for c in range(NIC):
            bank = c % 2
            w = [g_ve] if c == 0 else []
            if c >= 2:
                w = w + [sa2_ev[c - 2]]
            TE(lambda e, c=c, bank=bank: e.matmul(pp[bank][:, 0:TC], h128b_s[:],
                                                  agb[:, c * TC:(c + 1) * TC],
                                                  start=True, stop=False),
               waits=w)
            TE(lambda e, c=c, bank=bank: e.matmul(pp[bank][:, 0:TC], h128b_s[:],
                                                  agb[:, 16384 + c * TC:16384 + (c + 1) * TC],
                                                  start=False, stop=True),
               inc=mm)
            cm = mm.at()
            VE(lambda e, c=c, bank=bank: e.tensor_copy(
                out=aub[:, c * TC:(c + 1) * TC], in_=pp[bank][:, 0:TC]),
                waits=[cm])
            VE(lambda e, c=c, bank=bank: e.scalar_tensor_tensor(
                out=aub[:, 16384 + c * TC:16384 + (c + 1) * TC],
                in0=pp[bank][:, 0:TC], scalar=1.0, in1=aub[:, c * TC:(c + 1) * TC],
                op0=ALU.mult, op1=ALU.subtract), inc=ve)
            sa2_ev[c] = ve.at()
        VE(lambda e: e.drain(), inc=ve)
        u2_ve = ve.at()
        if taps and ti == 0:
            GD(lambda e: e.dma_start(out=tap["t_u2"][:], in_=arena[:, 0:16384]),
               waits=[u2_ve], inc=st)
        sb2_ev = {}
        CUSH("sync", [u2_ve])
        SY(lambda e: e.dma_start(out=u2d[:], in_=arena[:, 0:16384]), inc=ud)
        u2_ld = ud.at()
        g_gq64 = {}
        def em_gather64(q):
            fs = (q % 4) * 256
            w = [u2_ld] if q == 0 else []
            if q >= 4:
                w = w + [sb2_ev[q - 4]]
            SY(lambda e, q=q, fs=fs: e.dma_start(
                out=gb[:, fs:fs + 128],
                in_=u2d[2 * q:2 * q + 2, 0:8192].rearrange("s (c t) -> s c t", t=128)),
                waits=w, inc=gs[q % 4])
            SY(lambda e, q=q, fs=fs: e.dma_start(
                out=gb[:, fs + 128:fs + 256],
                in_=u2d[2 * q:2 * q + 2, 8192:16384].rearrange("s (c t) -> s c t", t=128)),
                inc=gs[q % 4])
            g_gq64[q] = gs[q % 4].at()
        for q in range(3):
            em_gather64(q)
        for q in range(NIC):
            slot = (q % 4) * 512
            if q + 3 < NIC:
                em_gather64(q + 3)
            wmm = [g_gq64[q]]
            if q >= 2:
                wmm = wmm + [sb2_ev[q - 2]]
            if q < 4:
                CUSH("tensor", [g_gq64[q]])
            TE(lambda e, q=q, slot=slot: e.matmul(pp[2 + q % 2][:, 0:TC], ih64b_s[:],
                                                  gbb[:, slot:slot + 256],
                                                  start=True, stop=False),
               waits=wmm)
            TE(lambda e, q=q, slot=slot: e.matmul(pp[2 + q % 2][:, 0:TC], ih64b_s[:],
                                                  gbb[:, slot + 256:slot + 512],
                                                  start=False, stop=True),
               inc=mm)
            qm = mm.at()
            VE(lambda e, q=q: e.tensor_copy(out=agb[:, q * TC:(q + 1) * TC],
                                            in_=pp[2 + q % 2][:, 0:TC]),
               waits=[qm])
            VE(lambda e, q=q: e.scalar_tensor_tensor(
                out=agb[:, 16384 + q * TC:16384 + (q + 1) * TC],
                in0=pp[2 + q % 2][:, 0:TC], scalar=1.0, in1=agb[:, q * TC:(q + 1) * TC],
                op0=ALU.mult, op1=ALU.subtract), inc=ve)
            sb2_ev[q] = ve.at()
        y2_ve = ve.at()
        if taps and ti == 0:
            GD(lambda e: e.dma_start(out=tap["t_y2"][:], in_=arena[:, A_KC:A_KC + 16384]),
               waits=[y2_ve], inc=st)
        wd_done = {}
        cache_ld = {}
        for q2 in range(NIC):
            par = q2 % 2
            boff = par * 4096
            w = [ug_ev[NIC - 1]] if q2 < 2 else [wd_done[q2 - 2]]
            if ti == 0:
                SY(lambda e, q2=q2: e.dma_start(out=wbuf[:, 4096:6144],
                                                in_=wdt[q2 * 128:(q2 + 1) * 128, :]),
                   waits=w + ([wd_done[q2 - 1]] if q2 >= 1 else []), inc=wf)
                wld = wf.at()
                VE(lambda e, boff=boff: e.tensor_copy(out=wbufb[:, boff:boff + 2048],
                                                      in_=wbuf[:, 4096:6144]),
                   waits=[wld] + ([cache_ld[q2 - 2]] if q2 >= 2 else []))
                VE(lambda e, boff=boff: e.scalar_tensor_tensor(
                    out=wbufb[:, boff + 2048:boff + 4096],
                    in0=wbuf[:, 4096:6144], scalar=1.0, in1=wbufb[:, boff:boff + 2048],
                    op0=ALU.mult, op1=ALU.subtract), inc=ve)
                wsp = ve.at()
            else:
                SY(lambda e, q2=q2, par=par: e.dma_start(
                    out=wbuf[:, par * 2048:par * 2048 + 2048], in_=wdb[q2][:]),
                   waits=w + ([wc_total] if ti == 1 and q2 < 2 else []), inc=ws[par])
                wsp = ws[par].at()
            for hc in range(16):
                wh = boff + hc * 128
                wl = boff + 2048 + hc * 128
                TE(lambda e, q2=q2, hc=hc, wh=wh: e.matmul(
                    pp[hc // 2][:, (hc % 2) * TC:(hc % 2 + 1) * TC],
                    wbufb[:, wh:wh + 128], agb[:, q2 * TC:(q2 + 1) * TC],
                    start=(q2 == 0 and hc % 2 == 0), stop=False,
                    skip_group_check=True),
                    waits=[wsp, y2_ve] if hc == 0 else ())
                TE(lambda e, q2=q2, hc=hc, wh=wh: e.matmul(
                    pp[hc // 2][:, (hc % 2) * TC:(hc % 2 + 1) * TC],
                    wbufb[:, wh:wh + 128], agb[:, 16384 + q2 * TC:16384 + (q2 + 1) * TC],
                    start=False, stop=False, skip_group_check=True))
                TE(lambda e, q2=q2, hc=hc, wl=wl: e.matmul(
                    pp[hc // 2][:, (hc % 2) * TC:(hc % 2 + 1) * TC],
                    wbufb[:, wl:wl + 128], agb[:, q2 * TC:(q2 + 1) * TC],
                    start=False, stop=(q2 == NIC - 1), skip_group_check=True),
                    inc=mm if hc == 15 else None)
            wd_done[q2] = mm.at()
            if ti == 0:
                SY(lambda e, q2=q2, par=par: e.dma_start(
                    out=wdb[q2][:], in_=wbuf[:, par * 2048:par * 2048 + 2048]),
                    waits=[wd_done[q2]], inc=wc)
                cache_ld[q2] = wc.at()
        dn_mm = mm.at()
        if ti == 0:
            wc_total = wc.at()
        for hc in range(16):
            VE(lambda e, hc=hc: e.scalar_tensor_tensor(
                out=arena[:, hc * TC:(hc + 1) * TC],
                in0=pp[hc // 2][:, (hc % 2) * TC:(hc % 2 + 1) * TC],
                scalar=wsd_s[:, hc:hc + 1], in1=xz[:, hc * TC:(hc + 1) * TC],
                op0=ALU.mult, op1=ALU.add),
                waits=[dn_mm] if hc == 0 else (), inc=ve if hc == 15 else None)
        out_ve = ve.at()
        if taps and ti == 0:
            GD(lambda e: e.dma_start(out=tap["t_out"][:], in_=arena[:, 0:4096]),
               waits=[out_ve], inc=st)
        CUSH("gpsimd", [out_ve])
        GD(lambda e, t0=t0: e.dma_start(
            out=y.rearrange("(c p) t -> p c t", p=128)[:, :, t0:t0 + TC],
            in_=arena[:, 0:4096].rearrange("p (c t) -> p c t", t=TC)),
            inc=st)
        prev_store = st.at()
        last_wq_done = wd_done[NIC - 1]

    final_st = st.at()
    P.add("gpsimd", lambda e: e.nop(), waits=[final_st])

    with nc.Block() as block:
        @block.sync
        def _(eng):
            P.replay("sync", eng)

        @block.tensor
        def _(eng):
            P.replay("tensor", eng)

        @block.vector
        def _(eng):
            P.replay("vector", eng)

        @block.scalar
        def _(eng):
            P.replay("scalar", eng)

        @block.gpsimd
        def _(eng):
            P.replay("gpsimd", eng)

    es.close()
    return nc


def _host_inputs(inputs):
    c = _host_consts()
    pug = _perm_ug()
    pdn = _perm_dn()
    wupt = np.ascontiguousarray(np.asarray(inputs["L0_ffn_W_up"]).T[pug, :])
    wgt = np.ascontiguousarray(np.asarray(inputs["L0_ffn_W_gate"]).T[pug, :])
    wdt = np.ascontiguousarray(np.asarray(inputs["L0_ffn_W_down"]).T[pdn, :])
    shared = {
        "wq": np.ascontiguousarray(np.asarray(inputs["L0_attn_Wq"])),
        "wk": np.ascontiguousarray(np.asarray(inputs["L0_attn_Wk"])),
        "wv": np.ascontiguousarray(np.asarray(inputs["L0_attn_Wv"])),
        "wo": np.ascontiguousarray(np.asarray(inputs["L0_attn_Wo"])),
        "wupt": wupt, "wgt": wgt, "wdt": wdt,
        "su_ug": np.ascontiguousarray(np.asarray(inputs["L0_ffn_SU_ug"])),
        "su_dp": np.ascontiguousarray(np.asarray(inputs["L0_ffn_SU_dp"])),
        "wsug": np.ascontiguousarray(np.asarray(inputs["L0_ffn_Wscale_ug"])),
        "wsdp": np.ascontiguousarray(np.asarray(inputs["L0_ffn_Wscale_dp"])),
        "cosT": c["cosT"], "sinF": c["sinF"],
        "h128": c["h128"], "ih16": c["ih16"], "ih64": c["ih64"],
        "ident": c["ident"], "mask": c["mask"], "onec": c["onec"], "oner": c["oner"],
        "h128b": c["h128"].astype(ml_dtypes.bfloat16),
        "ih64b": c["ih64"].astype(ml_dtypes.bfloat16),
        "ih16b": c["ih16"].astype(ml_dtypes.bfloat16),
    }
    x = np.asarray(inputs["L0_attn_x"])
    in_maps = []
    for i in range(8):
        m = dict(shared)
        m["xT"] = np.ascontiguousarray(x[i].T)
        in_maps.append(m)
    return in_maps


def kernel(**inputs):
    in_maps = _host_inputs(inputs)
    nc = build_program(taps=False)
    res = run_bass_kernel_spmd(nc, in_maps, core_ids=list(range(8)))
    out = np.stack([np.ascontiguousarray(r["y"].T) for r in res.results], axis=0)
    return out.astype(np.float32)
